# version 02
# baseline (speedup 1.0000x reference)
import math
from contextlib import ExitStack

import numpy as np
import concourse.bacc as bacc
import concourse.mybir as mybir
from concourse.bass_utils import run_bass_kernel_spmd

F32 = mybir.dt.float32
I16 = mybir.dt.int16
U32 = mybir.dt.uint32
AF = mybir.ActivationFunctionType
ALU = mybir.AluOpType
AXX = mybir.AxisListType.X

B = 4
T = 2048
NT = T // 128
D = 2048
ND = D // 128
DFF = 8192
NFF = DFF // 128
H = 16
HD = 128
CAP = 384
NCAP = CAP // 128
EPS = 1e-5

_start = 2.0 ** (-2.0 ** (-(math.log2(H) - 3)))
SLOPES = [_start ** (i + 1) for i in range(H)]

ENGS = ("sync", "scalar", "vector", "tensor", "gpsimd")


class _Stop(Exception):
    pass


class Sem:
    def __init__(self, stack, nc, name):
        self.h = stack.enter_context(nc.semaphore(name))
        self.n = 0

    def inc(self, instr, k=1):
        instr.then_inc(self.h, k)
        self.n += k
        return instr

    def dma(self, instr):
        instr.then_inc(self.h, 16)
        self.n += 16
        return instr

    def wait(self, eng, n=None):
        eng.wait_ge(self.h, self.n if n is None else n)


class Prog:

    def vstep(self, eng, ins):
        self.v.inc(ins)
        self.v.wait(eng)

    def __init__(self, stack, nc, blk):
        self.nc = nc
        self.blk = blk
        self.io = Sem(stack, nc, "s_io")
        self.z = Sem(stack, nc, "s_z")
        self.gd = Sem(stack, nc, "s_gd")
        self.t = Sem(stack, nc, "s_t")
        self.v = Sem(stack, nc, "s_v")
        self.s = Sem(stack, nc, "s_s")
        self.g = Sem(stack, nc, "s_g")
        self.bar = {e: Sem(stack, nc, f"bar_{e}") for e in ENGS}

    def barrier(self):
        for e in ENGS:
            bs = self.bar[e]

            @getattr(self.blk, e)
            def _(eng, bs=bs):
                bs.inc(eng.nop())
        sems = [self.io, self.z, self.gd, self.t, self.v, self.s, self.g] + [self.bar[e] for e in ENGS]
        for e in ENGS:
            @getattr(self.blk, e)
            def _(eng):
                for sm in sems:
                    sm.wait(eng)


def _transpose_seq(p, cb, jobs, consume):
    done = []
    pp = (cb["p_tr"], cb["p_at"])
    for k, src in enumerate(jobs):
        pbuf = pp[k % 2]

        @p.blk.tensor
        def _(t, k=k, src=src, pbuf=pbuf):
            if k >= 2:
                sm, n = done[k - 2]
                sm.wait(t, n)
            p.t.inc(t.transpose(pbuf[:, 0:128], src, cb["ident"][:, :]))
        done.append(consume(k, pbuf))


def _stats_tile(p, cb, xt_ap, out_mvr):
    blk = p.blk

    @blk.vector
    def _(v):
        p.vstep(v, v.tensor_tensor(cb["lnt"][:, :], xt_ap[:, :], xt_ap[:, :],
                                   op=ALU.mult))
        v.reduce_sum(cb["tmp"][:, 4:5], xt_ap[:, :], axis=AXX)
        p.vstep(v, v.reduce_sum(cb["tmp"][:, 5:6], cb["lnt"][:, :], axis=AXX))
        v.tensor_scalar(cb["aggr"][:, 0:1], cb["tmp"][:, 4:5], 1.0 / D, None,
                        op0=ALU.mult)
        p.vstep(v, v.tensor_scalar(cb["tmp"][:, 6:7], cb["tmp"][:, 5:6], 1.0 / D,
                                   None, op0=ALU.mult))
        p.vstep(v, v.tensor_tensor(cb["tmp"][:, 7:8], cb["aggr"][:, 0:1],
                                   cb["aggr"][:, 0:1], op=ALU.mult))
        p.v.inc(v.tensor_sub(cb["aggr"][:, 1:2], cb["tmp"][:, 6:7], cb["tmp"][:, 7:8]))

    @blk.scalar
    def _(s):
        p.v.wait(s)
        p.s.inc(s.activation(cb["tmp"][:, 0:1], cb["aggr"][:, 1:2], AF.Sqrt,
                             bias=cb["epst"][:, 0:1], scale=1.0))

    @blk.vector
    def _(v):
        p.s.wait(v)
        p.vstep(v, v.reciprocal(out_mvr[:, 1:2], cb["tmp"][:, 0:1]))
        p.vstep(v, v.tensor_copy(out_mvr[:, 0:1], cb["aggr"][:, 0:1]))
        p.vstep(v, v.tensor_tensor(cb["tmp"][:, 1:2], out_mvr[:, 0:1],
                                   out_mvr[:, 1:2], op=ALU.mult))
        p.v.inc(v.tensor_scalar(out_mvr[:, 2:3], cb["tmp"][:, 1:2], -1.0, None,
                                op0=ALU.mult))


def _score_tile(p, cb, xt_ap, mvr, sc_dst):
    blk = p.blk

    @blk.vector
    def _(v):
        p.vstep(v, v.tensor_tensor(cb["lnt"][:, :], xt_ap[:, :], cb["wr_rep"][:, :],
                                   op=ALU.mult))
        v.tensor_tensor(cb["tmp"][:, 3:4], mvr[:, 0:1], cb["sumwr"][:, 0:1], op=ALU.mult)
        p.vstep(v, v.reduce_sum(cb["tmp"][:, 2:3], cb["lnt"][:, :], axis=AXX))
        p.vstep(v, v.tensor_sub(cb["tmp"][:, 2:3], cb["tmp"][:, 2:3], cb["tmp"][:, 3:4]))
        p.vstep(v, v.tensor_tensor(cb["tmp"][:, 2:3], cb["tmp"][:, 2:3], mvr[:, 1:2],
                                   op=ALU.mult))
        p.v.inc(v.tensor_tensor(sc_dst, cb["tmp"][:, 2:3], cb["br128"][:, 0:1],
                                op=ALU.add))


def _emit_score_consts(p, cb, w_r, b_r):
    blk = p.blk

    @blk.sync
    def _(sy):
        p.io.dma(sy.dma_start(out=cb["wr_rep"][:, :],
                              in_=w_r[0:1, :].to_broadcast([128, D])))
        p.io.dma(sy.dma_start(out=cb["brt"][0:1, 0:1], in_=b_r[0:1, 0:1]))

    @blk.vector
    def _(v):
        p.io.wait(v)
        p.v.inc(v.reduce_sum(cb["sumwr"][:, 0:1], cb["wr_rep"][:, :], axis=AXX))

    @blk.gpsimd
    def _(g):
        p.io.wait(g)
        p.g.inc(g.partition_broadcast(cb["br128"][:, 0:1], cb["brt"][0:1, 0:1],
                                      channels=128))


def _emit_ln_pass(p, cb, x_src, n_tiles, sc_col0, lnx_dst=None, base_dst=None,
                  score=True):
    blk = p.blk
    for t in range(n_tiles):
        @blk.sync
        def _(sy, t=t):
            p.v.wait(sy)
            p.s.wait(sy)
            p.io.wait(sy)
            p.io.dma(sy.dma_start(out=cb["xt"][:, :],
                                  in_=x_src[t * 128:(t + 1) * 128, :]))

        @blk.vector
        def _(v):
            p.io.wait(v)
        _stats_tile(p, cb, cb["xt"], cb["mvr"])
        if score:
            _score_tile(p, cb, cb["xt"], cb["mvr"],
                        cb["sc_full"][:, sc_col0 + t: sc_col0 + t + 1])
        if lnx_dst is not None:
            @blk.vector
            def _(v):
                p.vstep(v, v.scalar_tensor_tensor(cb["lnt"][:, :], cb["xt"][:, :],
                                                  cb["mvr"][:, 1:2],
                                                  cb["mvr"][:, 2:3].to_broadcast([128, D]),
                                                  op0=ALU.mult, op1=ALU.add))
                p.v.inc(v.scalar_tensor_tensor(cb["xt"][:, :], cb["lnt"][:, :], 0.5,
                                               cb["xt"][:, :], op0=ALU.mult, op1=ALU.add))

            @blk.sync
            def _(sy, t=t):
                p.v.wait(sy)
                p.io.dma(sy.dma_start(out=lnx_dst[t * 128:(t + 1) * 128, :],
                                      in_=cb["lnt"][:, :]))
                p.io.dma(sy.dma_start(out=base_dst[t * 128:(t + 1) * 128, :],
                                      in_=cb["xt"][:, :]))


def _emit_common_consts(p, cb):
    blk = p.blk

    @blk.gpsimd
    def _(g):
        g.iota(cb["i128a"][:, :], pattern=[[0, 128]], base=0, channel_multiplier=1,
               allow_small_or_imprecise_dtypes=True)
        g.iota(cb["row512"][0:1, :], pattern=[[1, 512]], base=0,
               channel_multiplier=0, allow_small_or_imprecise_dtypes=True)
        p.g.inc(g.iota(cb["i128b"][:, :], pattern=[[1, 128]], base=0,
                       channel_multiplier=0, allow_small_or_imprecise_dtypes=True))

    @blk.vector
    def _(v):
        p.g.wait(v)
        v.tensor_scalar(cb["ones_row"][:, :], cb["row512"][0:1, :], -1.0, None,
                        op0=ALU.is_ge)
        v.memset(cb["zrow"][:, :], 0.0)
        p.vstep(v, v.tensor_scalar(cb["epst"][:, :], cb["i128a"][:, 0:1], 0.0, EPS,
                                   op0=ALU.mult, op1=ALU.add))
        p.v.inc(v.tensor_tensor(cb["ident"][:, :], cb["i128a"][:, :], cb["i128b"][:, :],
                                op=ALU.is_equal))


def _emit_ffn(p, cb, lnx_dram, out_dram, w1, b1col, w2, b2col, wrcol_dram, phase=99):
    nc, blk = p.nc, p.blk

    @blk.gpsimd
    def _(g):
        p.v.wait(g)
        p.io.wait(g)
        p.g.inc(g.kth_largest(cb["thr2"][:, :], cb["sc_full"][:, 0:2 * NT], 2 * NT,
                              510, quantile=0.85))
        p.g.inc(g.iota(cb["iota16"][:, :], pattern=[[16, T // 16]], base=1,
                       channel_multiplier=1, allow_small_or_imprecise_dtypes=True))
        p.g.inc(g.partition_broadcast(cb["thr16"][:, :], cb["thr2"][0:1, 1:2],
                                      channels=16))

    @blk.sync
    def _(sy):
        p.v.wait(sy)
        p.io.wait(sy)
        for q in range(8):
            p.io.dma(sy.dma_start(out=cb["sc16"][:, q::8],
                                  in_=cb["sc_full"][16 * q:16 * (q + 1), 0:NT],
                                  allow_slow_non_contiguous=True))

    p.barrier()

    @blk.vector
    def _(v):
        v.memset(cb["cand"][:, T // 16:], float(T))
        v.memset(cb["idx16"][:, :], 0)
        p.vstep(v, v.tensor_scalar(cb["mask16"][:, :], cb["sc16"][:, :],
                                   cb["thr16"][:, 0:1], None, op0=ALU.is_gt))
        p.vstep(v, v.tensor_tensor(cb["cand"][:, 0:T // 16], cb["mask16"][:, :],
                                   cb["iota16"][:, :], op=ALU.mult))
        p.v.inc(v.tensor_scalar_add(cb["cand"][:, 0:T // 16],
                                    cb["cand"][:, 0:T // 16], -1.0))

    @blk.gpsimd
    def _(g):
        p.v.wait(g)
        p.g.inc(g.sparse_gather(cb["idxc"][:, :], cb["cand"][:, :],
                                num_found=cb["nf"][:, :]))

    @blk.vector
    def _(v):
        p.g.wait(v)
        p.v.inc(v.tensor_copy(cb["idx16"][0:16, :], cb["idxc"][:, 0:CAP // 16]))

    @blk.sync
    def _(sy):
        p.v.wait(sy)
        for k in range(1, 8):
            p.io.dma(sy.dma_start(out=cb["idx16"][16 * k:16 * (k + 1), :],
                                  in_=cb["idx16"][0:16, :]))

    p.barrier()
    if phase < 3:
        return

    with ExitStack() as ost:
        osb = lambda name, shape: ost.enter_context(nc.sbuf_tensor(name, shape, F32))
        xgT = osb("xgT", [128, ND * CAP])
        wrcol = osb("wrcol", [128, ND])
        b1t = osb("b1t", [128, NFF])
        b2t = osb("b2t", [128, ND])
        wg05 = osb("wg05", [128, NCAP])

        @blk.sync
        def _(sy):
            p.io.dma(sy.dma_start(out=wrcol[:, :], in_=wrcol_dram[:, :]))
            p.io.dma(sy.dma_start(out=b1t[:, :], in_=b1col[:, :]))
            p.io.dma(sy.dma_start(out=b2t[:, :], in_=b2col[:, :]))

        with ExitStack() as gst:
            xg = gst.enter_context(nc.sbuf_tensor("xg", [128, NCAP, D], F32))

            @blk.gpsimd
            def _(g):
                p.io.wait(g)
                p.gd.dma(g.dma_gather(xg[:, :, :], lnx_dram[:, :], cb["idx16"][:, :],
                                      CAP, CAP, D))

            p.barrier()
            if phase == 31:
                raise _Stop

            jobs = [xg[:, s, i * 128:(i + 1) * 128]
                    for s in range(NCAP) for i in range(ND)]

            def consume(k, pbuf):
                s4, i = divmod(k, ND)

                @blk.vector
                def _(v, s4=s4, i=i, pbuf=pbuf):
                    p.t.wait(v)
                    p.v.inc(v.tensor_copy(
                        xgT[:, i * CAP + s4 * 128: i * CAP + (s4 + 1) * 128],
                        pbuf[:, 0:128]))
                return (p.v, p.v.n)

            _transpose_seq(p, cb, jobs, consume)
            p.barrier()
            if phase == 32:
                raise _Stop

            @blk.vector
            def _(v):
                for s4 in range(NCAP):
                    p.v.inc(v.tensor_scalar(xg[:, s4, :], xg[:, s4, :], -0.5, None,
                                            op0=ALU.mult))

            p.barrier()

            @blk.gpsimd
            def _(g):
                p.gd.dma(g.dma_scatter_add(out_dram[:, :], xg[:, :, :],
                                           cb["idx16"][:, :], CAP, CAP, D))

            p.barrier()
            if phase == 33:
                raise _Stop

        for s4 in range(NCAP):
            @blk.tensor
            def _(t, s4=s4):
                if s4 > 0:
                    p.v.wait(t)
                for i in range(ND):
                    ins = t.matmul(cb["p_tr"][:, 0:1],
                                   xgT[:, i * CAP + s4 * 128: i * CAP + (s4 + 1) * 128],
                                   wrcol[:, i:i + 1],
                                   start=(i == 0), stop=(i == ND - 1))
                p.t.inc(ins)

            @blk.vector
            def _(v, s4=s4):
                p.t.wait(v)
                p.v.inc(v.tensor_scalar(wg05[:, s4:s4 + 1], cb["p_tr"][:, 0:1],
                                        cb["br128"][:, 0:1], 0.5,
                                        op0=ALU.add, op1=ALU.mult))

        p.barrier()
        if phase == 34:
            raise _Stop
        if phase < 4:
            return

        with ExitStack() as hst:
            hbuf = hst.enter_context(nc.sbuf_tensor("hbuf", [128, NFF * CAP], F32))

            with ExitStack() as wst:
                w1buf = wst.enter_context(nc.sbuf_tensor("w1buf", [128, ND * 128], F32))
                tcnt = {}
                scnt = {}
                for j in range(NFF):
                    @blk.sync
                    def _(sy, j=j):
                        if j >= 1:
                            p.t.wait(sy, tcnt[j - 1])
                        for i in range(ND):
                            p.z.dma(sy.dma_start(
                                out=w1buf[:, i * 128:(i + 1) * 128],
                                in_=w1[i * 128:(i + 1) * 128, j * 128:(j + 1) * 128]))

                    @blk.tensor
                    def _(t, j=j):
                        p.z.wait(t)
                        if j >= 1:
                            p.s.wait(t, scnt[j - 1])
                        for i in range(ND):
                            ins = t.matmul(cb["p_h"][:, 0:CAP],
                                           w1buf[:, i * 128:(i + 1) * 128],
                                           xgT[:, i * CAP:(i + 1) * CAP],
                                           start=(i == 0), stop=(i == ND - 1))
                        p.t.inc(ins)
                        tcnt[j] = p.t.n

                    @blk.scalar
                    def _(s, j=j):
                        p.t.wait(s, tcnt[j])
                        p.s.inc(s.activation(hbuf[:, j * CAP:(j + 1) * CAP],
                                             cb["p_h"][:, 0:CAP], AF.Gelu_apprx_tanh,
                                             bias=b1t[:, j:j + 1], scale=1.0))
                        scnt[j] = p.s.n
                p.barrier()
            if phase < 5:
                return

            with ExitStack() as wst2:
                w2buf = wst2.enter_context(nc.sbuf_tensor("w2buf", [128, 6 * 128], F32))
                blkF = wst2.enter_context(nc.sbuf_tensor("blkF", [128, 6 * CAP], F32))
                vtile = wst2.enter_context(nc.sbuf_tensor("vtile", [128, NCAP, D], F32))
                pbanks = [cb["p_a"], cb["p_b"], cb["p_c"], cb["p_d"], cb["p_h"],
                          cb["p_sc"]]
                dgroups = [list(range(0, 6)), list(range(6, 12)), list(range(12, 16))]
                for dgrp in dgroups:
                    w2cnt = {}
                    for j in range(NFF):
                        @blk.scalar
                        def _(s, j=j, dgrp=dgrp):
                            if j >= 1:
                                p.t.wait(s, w2cnt[j - 1])
                            p.z.dma(s.dma_start(
                                out=w2buf[:, 0: len(dgrp) * 128],
                                in_=w2[j * 128:(j + 1) * 128,
                                       dgrp[0] * 128:(dgrp[-1] + 1) * 128]))

                        @blk.tensor
                        def _(t, j=j, dgrp=dgrp):
                            p.z.wait(t)
                            for di in range(len(dgrp)):
                                ins = t.matmul(pbanks[di][:, 0:CAP],
                                               w2buf[:, di * 128:(di + 1) * 128],
                                               hbuf[:, j * CAP:(j + 1) * CAP],
                                               start=(j == 0), stop=(j == NFF - 1))
                            p.t.inc(ins)
                            w2cnt[j] = p.t.n

                    @blk.vector
                    def _(v, dgrp=dgrp):
                        p.t.wait(v)
                        for di, dc in enumerate(dgrp):
                            p.v.inc(v.tensor_scalar(blkF[:, di * CAP:(di + 1) * CAP],
                                                    pbanks[di][:, 0:CAP],
                                                    b2t[:, dc:dc + 1], None,
                                                    op0=ALU.add))

                    p.barrier()

                    jobs = [blkF[:, di * CAP + s4 * 128: di * CAP + (s4 + 1) * 128]
                            for di in range(len(dgrp)) for s4 in range(NCAP)]

                    def consume(k, pbuf, dgrp=dgrp):
                        di, s4 = divmod(k, NCAP)
                        dc = dgrp[di]

                        @blk.vector
                        def _(v, s4=s4, dc=dc, pbuf=pbuf):
                            p.t.wait(v)
                            p.v.inc(v.tensor_scalar(
                                vtile[:, s4, dc * 128:(dc + 1) * 128],
                                pbuf[:, 0:128], wg05[:, s4:s4 + 1], None,
                                op0=ALU.mult))
                        return (p.v, p.v.n)

                    _transpose_seq(p, cb, jobs, consume)
                    p.barrier()

                @blk.gpsimd
                def _(g):
                    p.gd.dma(g.dma_scatter_add(out_dram[:, :], vtile[:, :, :],
                                               cb["idx16"][:, :], CAP, CAP, D))
                p.barrier()


def _alloc_core(nc, st):
    sb = lambda name, shape, dt=F32: st.enter_context(nc.sbuf_tensor(name, shape, dt))
    pst = lambda name, shape: st.enter_context(nc.psum_tensor(name, shape, F32))
    cb = {}
    cb["xt"] = sb("xt", [128, D])
    cb["lnt"] = sb("lnt", [128, D])
    cb["stats"] = sb("stats", [128, 24])
    cb["aggr"] = sb("aggr", [128, 2])
    cb["mvr"] = sb("mvr", [128, 3])
    cb["tmp"] = sb("tmp", [128, 8])
    cb["wr_rep"] = sb("wr_rep", [128, D])
    cb["sumwr"] = sb("sumwr", [128, 1])
    cb["brt"] = sb("brt", [1, 1])
    cb["br128"] = sb("br128", [128, 1])
    cb["ones_row"] = sb("ones_row", [1, 512])
    cb["zrow"] = sb("zrow", [1, D])
    cb["ident"] = sb("ident", [128, 128])
    cb["epst"] = sb("epst", [128, 1])
    cb["row512"] = sb("row512", [1, 512])
    cb["i128a"] = sb("i128a", [128, 128])
    cb["i128b"] = sb("i128b", [128, 128])
    cb["sc_full"] = sb("sc_full", [128, 2 * NT])
    cb["sc16"] = sb("sc16", [16, T // 16])
    cb["thr2"] = sb("thr2", [128, 2])
    cb["thr16"] = sb("thr16", [16, 1])
    cb["iota16"] = sb("iota16", [16, T // 16])
    cb["cand"] = sb("cand", [16, (T + CAP) // 16])
    cb["mask16"] = sb("mask16", [16, T // 16])
    cb["idxc"] = sb("idxc", [16, 56])
    cb["nf"] = sb("nf", [1, 1], U32)
    cb["idx16"] = sb("idx16", [128, CAP // 16], I16)
    cb["p_a"] = pst("p_a", [128, 512])
    cb["p_b"] = pst("p_b", [128, 512])
    cb["p_c"] = pst("p_c", [128, 512])
    cb["p_d"] = pst("p_d", [128, 512])
    cb["p_h"] = pst("p_h", [128, 512])
    cb["p_sc"] = pst("p_sc", [128, 512])
    cb["p_tr"] = pst("p_tr", [128, 128])
    cb["p_at"] = pst("p_at", [128, 128])
    return cb


def _alloc_attn(nc, st, cb, full=True):
    sb = lambda name, shape: st.enter_context(nc.sbuf_tensor(name, shape, F32))
    cb["dq_base"] = sb("dq_base", [128, 128])
    cb["c128"] = sb("c128", [128, 1])
    cb["dq_b"] = sb("dq_b", [128, H * 128])
    cb["lamc_c"] = sb("lamc_c", [128, H])
    cb["Sst"] = sb("Sst", [128, H * HD])
    cb["qch"] = sb("qch", [128, H * 128])
    cb["atch"] = sb("atch", [128, H * 128])
    cb["qdT"] = sb("qdT", [128, 128])
    if full:
        cb["diffT"] = sb("diffT", [128, 128])
        cb["causalT"] = sb("causalT", [128, 128])
        cb["dk_base"] = sb("dk_base", [128, 1])
        cb["dmaskT"] = sb("dmaskT", [128, H * 128])
        cb["dk_c"] = sb("dk_c", [128, H])
        cb["kch"] = sb("kch", [128, D])
        cb["vch"] = sb("vch", [128, D])
        cb["kTh"] = sb("kTh", [128, 128])
        cb["scT"] = sb("scT", [128, 128])
        cb["kdh"] = sb("kdh", [128, 128])


def _emit_attn_consts(p, cb, full=True):
    blk = p.blk

    @blk.gpsimd
    def _(g):
        if full:
            g.iota(cb["diffT"][:, :], pattern=[[1, 128]], base=0, channel_multiplier=-1,
                   allow_small_or_imprecise_dtypes=True)
            g.iota(cb["dk_base"][:, 0:1], pattern=[[0, 1]], base=127,
                   channel_multiplier=-1, allow_small_or_imprecise_dtypes=True)
        p.g.inc(g.iota(cb["dq_base"][:, :], pattern=[[1, 128]], base=1,
                       channel_multiplier=0, allow_small_or_imprecise_dtypes=True))

    @blk.vector
    def _(v):
        p.g.wait(v)
        p.v.inc(v.tensor_scalar(cb["c128"][:, 0:1], cb["dq_base"][:, 0:1], 0.0, 128.0,
                                op0=ALU.mult, op1=ALU.add))

    if full:
        @blk.vector
        def _(v):
            p.g.wait(v)
            p.vstep(v, v.tensor_scalar(cb["causalT"][:, :], cb["diffT"][:, :], 0.0,
                                       None, op0=ALU.is_ge))
            p.v.inc(v.tensor_scalar(cb["diffT"][:, :], cb["diffT"][:, :], 0.0, None,
                                    op0=ALU.max))

    @blk.scalar
    def _(s):
        p.g.wait(s)
        p.v.wait(s)
        for hh in range(H):
            sl = SLOPES[hh]
            if full:
                s.activation(cb["dmaskT"][:, hh * 128:(hh + 1) * 128], cb["diffT"][:, :],
                             AF.Exp, scale=-sl)
                s.activation(cb["dk_c"][:, hh:hh + 1], cb["dk_base"][:, 0:1],
                             AF.Exp, scale=-sl)
            s.activation(cb["dq_b"][:, hh * 128:(hh + 1) * 128], cb["dq_base"][:, :],
                         AF.Exp, scale=-sl)
            ins = s.activation(cb["lamc_c"][:, hh:hh + 1], cb["c128"][:, 0:1],
                               AF.Exp, scale=-sl)
        p.s.inc(ins)

    if full:
        @blk.vector
        def _(v):
            p.s.wait(v)
            for hh in range(H):
                ins = v.tensor_tensor(cb["dmaskT"][:, hh * 128:(hh + 1) * 128],
                                      cb["dmaskT"][:, hh * 128:(hh + 1) * 128],
                                      cb["causalT"][:, :], op=ALU.mult)
            p.v.inc(ins)


def build_L1(phase=99):
    nc = bacc.Bacc("TRN2", detect_race_conditions=False)
    x0_my = nc.dram_tensor("x0_my", [T, D], F32, kind="ExternalInput").ap()
    x0_pr = nc.dram_tensor("x0_pr", [T, D], F32, kind="ExternalInput").ap()
    wqkvT = nc.dram_tensor("wqkvT", [D, 3 * D], F32, kind="ExternalInput").ap()
    bq_col = nc.dram_tensor("bq_col", [128, ND], F32, kind="ExternalInput").ap()
    bkv_row = nc.dram_tensor("bkv_row", [1, 2 * D], F32, kind="ExternalInput").ap()
    w_r = nc.dram_tensor("w_r", [1, D], F32, kind="ExternalInput").ap()
    b_r = nc.dram_tensor("b_r", [1, 1], F32, kind="ExternalInput").ap()
    wrcol_d = nc.dram_tensor("wrcol_d", [128, ND], F32, kind="ExternalInput").ap()
    w1 = nc.dram_tensor("w1", [D, DFF], F32, kind="ExternalInput").ap()
    b1col = nc.dram_tensor("b1col", [128, NFF], F32, kind="ExternalInput").ap()
    w2 = nc.dram_tensor("w2", [DFF, D], F32, kind="ExternalInput").ap()
    b2col = nc.dram_tensor("b2col", [128, ND], F32, kind="ExternalInput").ap()

    lnx1 = nc.dram_tensor("lnx1", [T + 1, D], F32, kind="Internal").ap()
    x1 = nc.dram_tensor("x1", [T + 1, D], F32, kind="ExternalOutput").ap()
    q_out = nc.dram_tensor("q_out", [D, T], F32, kind="ExternalOutput").ap()
    k_dram = nc.dram_tensor("k_dram", [T, D], F32, kind="Internal").ap()
    v_dram = nc.dram_tensor("v_dram", [T, D], F32, kind="Internal").ap()
    attn_out = nc.dram_tensor("attn_out", [D, T], F32, kind="ExternalOutput").ap()
    s16_out = nc.dram_tensor("s16_out", [H * HD, HD], F32, kind="ExternalOutput").ap()
    sc_out = nc.dram_tensor("sc_out", [128, NT], F32, kind="ExternalOutput").ap()

    try:
      with ExitStack() as st, nc.Block() as blk:
        cb = _alloc_core(nc, st)
        p = Prog(st, nc, blk)
        _emit_common_consts(p, cb)
        _emit_score_consts(p, cb, w_r, b_r)
        p.barrier()

        @blk.sync
        def _(sy):
            p.io.dma(sy.dma_start(out=lnx1[T:T + 1, :], in_=cb["zrow"][0:1, :]))
            p.io.dma(sy.dma_start(out=x1[T:T + 1, :], in_=cb["zrow"][0:1, :]))

        _emit_ln_pass(p, cb, x0_my, NT, 0, lnx_dst=lnx1, base_dst=x1, score=True)
        _emit_ln_pass(p, cb, x0_pr, NT, NT, score=True)
        p.barrier()
        if phase < 2:
            raise _Stop

        @blk.sync
        def _(sy):
            p.io.dma(sy.dma_start(out=sc_out[:, :], in_=cb["sc_full"][:, 0:NT]))

        _emit_ffn(p, cb, lnx1, x1, w1, b1col, w2, b2col, wrcol_d, phase=phase)
        if phase < 6:
            raise _Stop

        with ExitStack() as qst:
            qal = lambda name, shape: qst.enter_context(nc.sbuf_tensor(name, shape, F32))
            lnxT = qal("lnxT", [128, ND * 1024])
            wstrip = qal("wstrip", [128, ND * 512])
            bqs = qal("bqs", [128, ND])
            bkvs = qal("bkvs", [1, 2 * D])
            osb = qal("osb", [128, 512])

            @blk.sync
            def _(sy):
                p.io.dma(sy.dma_start(out=bqs[:, :], in_=bq_col[:, :]))
                p.io.dma(sy.dma_start(out=bkvs[0:1, :], in_=bkv_row[0:1, :]))

            for th in range(2):
                for tl in range(8):
                    t_g = th * 8 + tl

                    @blk.sync
                    def _(sy, t_g=t_g):
                        p.v.wait(sy)
                        p.t.wait(sy)
                        p.io.wait(sy)
                        p.io.dma(sy.dma_start(out=cb["xt"][:, :],
                                              in_=x1[t_g * 128:(t_g + 1) * 128, :]))

                    @blk.vector
                    def _(v):
                        p.io.wait(v)
                    _stats_tile(p, cb, cb["xt"], cb["mvr"])

                    @blk.vector
                    def _(v):
                        p.v.inc(v.scalar_tensor_tensor(
                            cb["lnt"][:, :], cb["xt"][:, :], cb["mvr"][:, 1:2],
                            cb["mvr"][:, 2:3].to_broadcast([128, D]),
                            op0=ALU.mult, op1=ALU.add))

                    jobs = [cb["lnt"][:, i * 128:(i + 1) * 128] for i in range(ND)]

                    def consume(k, pbuf, tl=tl):
                        @blk.vector
                        def _(v, k=k, pbuf=pbuf, tl=tl):
                            p.t.wait(v)
                            p.v.inc(v.tensor_copy(
                                lnxT[:, k * 1024 + tl * 128: k * 1024 + (tl + 1) * 128],
                                pbuf[:, 0:128]))
                        return (p.v, p.v.n)

                    @blk.tensor
                    def _(t):
                        p.v.wait(t)
                    _transpose_seq(p, cb, jobs, consume)

                p.barrier()

                for ofc in range(ND):
                    @blk.scalar
                    def _(s, ofc=ofc):
                        p.t.wait(s)
                        for dc in range(ND):
                            p.z.dma(s.dma_start(
                                out=wstrip[:, dc * 512: dc * 512 + 128],
                                in_=wqkvT[dc * 128:(dc + 1) * 128,
                                          ofc * 128:(ofc + 1) * 128]))

                    for g2 in range(2):
                        @blk.tensor
                        def _(t, ofc=ofc, g2=g2):
                            p.z.wait(t)
                            p.s.wait(t)
                            for dc in range(ND):
                                ins = t.matmul(cb["p_a"][:, :],
                                               wstrip[:, dc * 512: dc * 512 + 128],
                                               lnxT[:, dc * 1024 + g2 * 512:
                                                    dc * 1024 + (g2 + 1) * 512],
                                               start=(dc == 0), stop=(dc == ND - 1))
                            p.t.inc(ins)

                        @blk.vector
                        def _(v, ofc=ofc, g2=g2):
                            p.t.wait(v)
                            p.io.wait(v)
                            p.s.inc(v.tensor_scalar(osb[:, :], cb["p_a"][:, :],
                                                    bqs[:, ofc:ofc + 1], None,
                                                    op0=ALU.add))

                        @blk.sync
                        def _(sy, ofc=ofc, g2=g2, th=th):
                            p.s.wait(sy)
                            p.io.dma(sy.dma_start(
                                out=q_out[ofc * 128:(ofc + 1) * 128,
                                          th * 1024 + g2 * 512:
                                          th * 1024 + (g2 + 1) * 512],
                                in_=osb[:, :]))

                for sp in range(8):
                    @blk.scalar
                    def _(s, sp=sp):
                        p.t.wait(s)
                        for dc in range(ND):
                            p.z.dma(s.dma_start(
                                out=wstrip[:, dc * 512:(dc + 1) * 512],
                                in_=wqkvT[dc * 128:(dc + 1) * 128,
                                          D + sp * 512: D + (sp + 1) * 512]))

                    for tl in range(8):
                        @blk.tensor
                        def _(t, sp=sp, tl=tl):
                            p.z.wait(t)
                            p.s.wait(t)
                            for dc in range(ND):
                                t.matmul(cb["p_b"][:, :],
                                         lnxT[:, dc * 1024 + tl * 128:
                                              dc * 1024 + (tl + 1) * 128],
                                         wstrip[:, dc * 512:(dc + 1) * 512],
                                         start=(dc == 0), stop=False)
                            p.t.inc(t.matmul(cb["p_b"][:, :], cb["ones_row"][0:1, 0:128],
                                             bkvs[0:1, sp * 512:(sp + 1) * 512],
                                             start=False, stop=True))

                        @blk.scalar
                        def _(s, sp=sp, tl=tl):
                            p.t.wait(s)
                            p.io.wait(s)
                            p.s.inc(s.copy(osb[:, :], cb["p_b"][:, :]))

                        @blk.sync
                        def _(sy, sp=sp, tl=tl, th=th):
                            p.s.wait(sy)
                            row = (th * 8 + tl) * 128
                            dst = k_dram if sp < 4 else v_dram
                            col = (sp % 4) * 512
                            p.io.dma(sy.dma_start(out=dst[row:row + 128, col:col + 512],
                                                  in_=osb[:, :]))

                p.barrier()
        if phase < 7:
            raise _Stop

        with ExitStack() as ast:
            _alloc_attn(nc, ast, cb, full=True)
            _emit_attn_consts(p, cb, full=True)

            @blk.vector
            def _(v):
                p.v.inc(v.tensor_scalar(cb["Sst"][:, :], cb["xt"][:, :], 0.0, None,
                                        op0=ALU.mult))
            p.barrier()

            for c in range(NT):
                @blk.sync
                def _(sy, c=c):
                    p.v.wait(sy)
                    p.t.wait(sy)
                    p.io.dma(sy.dma_start(out=cb["kch"][:, :],
                                          in_=k_dram[c * 128:(c + 1) * 128, :]))
                    p.io.dma(sy.dma_start(out=cb["vch"][:, :],
                                          in_=v_dram[c * 128:(c + 1) * 128, :]))
                    for hh in range(H):
                        p.io.dma(sy.dma_start(
                            out=cb["qch"][:, hh * 128:(hh + 1) * 128],
                            in_=q_out[hh * 128:(hh + 1) * 128, c * 128:(c + 1) * 128]))

                for hh in range(H):
                    hs = slice(hh * 128, (hh + 1) * 128)

                    @blk.tensor
                    def _(t, hs=hs):
                        p.io.wait(t)
                        p.v.wait(t)
                        p.t.inc(t.transpose(cb["p_tr"][:, 0:128], cb["kch"][:, hs],
                                            cb["ident"][:, :]))

                    @blk.vector
                    def _(v, hs=hs):
                        p.t.wait(v)
                        v.tensor_copy(cb["kTh"][:, :], cb["p_tr"][:, 0:128])
                        p.v.inc(v.tensor_tensor(cb["qdT"][:, :], cb["qch"][:, hs],
                                                cb["dq_b"][:, hs], op=ALU.mult))

                    @blk.vector
                    def _(v, hh=hh, hs=hs):
                        p.s.inc(v.tensor_scalar(cb["kdh"][:, :], cb["kch"][:, hs],
                                                cb["dk_c"][:, hh:hh + 1], None,
                                                op0=ALU.mult))

                    @blk.tensor
                    def _(t, hs=hs):
                        p.v.wait(t)
                        p.s.wait(t)
                        p.t.inc(t.matmul(cb["p_at"][:, 0:128], cb["kTh"][:, :],
                                         cb["qch"][:, hs], start=True, stop=True))

                    @blk.vector
                    def _(v, hs=hs):
                        p.t.wait(v)
                        p.v.inc(v.tensor_tensor(cb["scT"][:, :], cb["p_at"][:, 0:128],
                                                cb["dmaskT"][:, hs], op=ALU.mult))

                    @blk.tensor
                    def _(t, hs=hs):
                        p.v.wait(t)
                        t.matmul(cb["p_tr"][:, 0:128], cb["Sst"][:, hs], cb["qdT"][:, :],
                                 start=True, stop=False)
                        t.matmul(cb["p_tr"][:, 0:128], cb["vch"][:, hs], cb["scT"][:, :],
                                 start=False, stop=True)
                        p.t.inc(t.matmul(cb["p_at"][:, 0:128], cb["kdh"][:, :],
                                         cb["vch"][:, hs], start=True, stop=True))

                    @blk.vector
                    def _(v, hs=hs, hh=hh):
                        p.t.wait(v)
                        v.tensor_copy(cb["atch"][:, hs], cb["p_tr"][:, 0:128])
                        p.v.inc(v.scalar_tensor_tensor(cb["Sst"][:, hs], cb["Sst"][:, hs],
                                                       cb["lamc_c"][:, hh:hh + 1],
                                                       cb["p_at"][:, 0:128],
                                                       op0=ALU.mult, op1=ALU.add))

                @blk.sync
                def _(sy, c=c):
                    p.v.wait(sy)
                    for hh in range(H):
                        p.io.dma(sy.dma_start(
                            out=attn_out[hh * 128:(hh + 1) * 128, c * 128:(c + 1) * 128],
                            in_=cb["atch"][:, hh * 128:(hh + 1) * 128]))

            p.barrier()

            @blk.sync
            def _(sy):
                for hh in range(H):
                    p.io.dma(sy.dma_start(out=s16_out[hh * 128:(hh + 1) * 128, :],
                                          in_=cb["Sst"][:, hh * HD:(hh + 1) * HD]))

        p.barrier()
    except _Stop:
        pass
    nc.finalize()
    return nc


def build_L2():
    nc = bacc.Bacc("TRN2", detect_race_conditions=False)
    x1 = nc.dram_tensor("x1", [T + 1, D], F32, kind="ExternalInput").ap()
    attn_in = nc.dram_tensor("attn_in", [D, T], F32, kind="ExternalInput").ap()
    q_in = nc.dram_tensor("q_in", [D, T], F32, kind="ExternalInput").ap()
    s16_in = nc.dram_tensor("s16_in", [H * HD, HD], F32, kind="ExternalInput").ap()
    woutT = nc.dram_tensor("woutT", [D, D], F32, kind="ExternalInput").ap()
    b_out = nc.dram_tensor("b_out", [1, D], F32, kind="ExternalInput").ap()
    w_r = nc.dram_tensor("w_r", [1, D], F32, kind="ExternalInput").ap()
    b_r = nc.dram_tensor("b_r", [1, 1], F32, kind="ExternalInput").ap()

    x2_out = nc.dram_tensor("x2_out", [T + 1, D], F32, kind="ExternalOutput").ap()
    sc3_out = nc.dram_tensor("sc3_out", [128, NT], F32, kind="ExternalOutput").ap()

    with ExitStack() as st, nc.Block() as blk:
        cb = _alloc_core(nc, st)
        p = Prog(st, nc, blk)
        _emit_common_consts(p, cb)
        _emit_score_consts(p, cb, w_r, b_r)
        _alloc_attn(nc, st, cb, full=False)
        _emit_attn_consts(p, cb, full=False)
        sb = lambda name, shape: st.enter_context(nc.sbuf_tensor(name, shape, F32))
        wouts = sb("wouts", [128, ND * 1024])
        lamcp = sb("lamcp", [128, H * NT])
        c128g = sb("c128g", [128, NT])
        bos = sb("bos", [1, D])
        x1s = sb("x1s", [128, 512])
        x2s = sb("x2s", [128, 512])
        at2 = sb("at2", [128, H * 128])

        @blk.gpsimd
        def _(g):
            p.g.inc(g.iota(c128g[:, :], pattern=[[128, NT]], base=0,
                           channel_multiplier=0, allow_small_or_imprecise_dtypes=True))

        @blk.scalar
        def _(s):
            p.g.wait(s)
            for hh in range(H):
                ins = s.activation(lamcp[:, hh * NT:(hh + 1) * NT], c128g[:, :],
                                   AF.Exp, scale=-SLOPES[hh])
            p.s.inc(ins)

        @blk.sync
        def _(sy):
            p.io.dma(sy.dma_start(out=bos[0:1, :], in_=b_out[0:1, :]))
            for hh in range(H):
                p.io.dma(sy.dma_start(out=cb["Sst"][:, hh * HD:(hh + 1) * HD],
                                      in_=s16_in[hh * 128:(hh + 1) * 128, :]))
            p.io.dma(sy.dma_start(out=x2_out[T:T + 1, :], in_=cb["zrow"][0:1, :]))

        p.barrier()

        attn2 = nc.dram_tensor("attn2", [D, T], F32, kind="Internal").ap()

        for c in range(NT):
            @blk.sync
            def _(sy, c=c):
                p.v.wait(sy)
                p.io.wait(sy)
                for hh in range(H):
                    p.io.dma(sy.dma_start(
                        out=cb["qch"][:, hh * 128:(hh + 1) * 128],
                        in_=q_in[hh * 128:(hh + 1) * 128, c * 128:(c + 1) * 128]))
                    p.io.dma(sy.dma_start(
                        out=cb["atch"][:, hh * 128:(hh + 1) * 128],
                        in_=attn_in[hh * 128:(hh + 1) * 128, c * 128:(c + 1) * 128]))

            for hh in range(H):
                hs = slice(hh * 128, (hh + 1) * 128)

                @blk.vector
                def _(v, hs=hs):
                    p.io.wait(v)
                    p.t.wait(v)
                    p.v.inc(v.tensor_tensor(cb["qdT"][:, :], cb["qch"][:, hs],
                                            cb["dq_b"][:, hs], op=ALU.mult))

                @blk.tensor
                def _(t, hs=hs):
                    p.v.wait(t)
                    p.t.inc(t.matmul(cb["p_at"][:, 0:128], cb["Sst"][:, hs],
                                     cb["qdT"][:, :], start=True, stop=True))

                @blk.vector
                def _(v, c=c, hh=hh, hs=hs):
                    p.t.wait(v)
                    p.v.inc(v.scalar_tensor_tensor(
                        at2[:, hs], cb["p_at"][:, 0:128],
                        lamcp[:, hh * NT + c: hh * NT + c + 1],
                        cb["atch"][:, hs], op0=ALU.mult, op1=ALU.add))

            @blk.sync
            def _(sy, c=c):
                p.v.wait(sy)
                for hh in range(H):
                    p.io.dma(sy.dma_start(
                        out=attn2[hh * 128:(hh + 1) * 128, c * 128:(c + 1) * 128],
                        in_=at2[:, hh * 128:(hh + 1) * 128]))

        p.barrier()

        for dh in range(2):
            @blk.sync
            def _(sy, dh=dh):
                p.t.wait(sy)
                for dc in range(ND):
                    p.io.dma(sy.dma_start(
                        out=wouts[:, dc * 1024:(dc + 1) * 1024],
                        in_=woutT[dc * 128:(dc + 1) * 128,
                                  dh * 1024:(dh + 1) * 1024]))
            p.barrier()

            for c in range(NT):
                @blk.sync
                def _(sy, c=c):
                    p.v.wait(sy)
                    for hh in range(H):
                        p.io.dma(sy.dma_start(
                            out=cb["atch"][:, hh * 128:(hh + 1) * 128],
                            in_=attn2[hh * 128:(hh + 1) * 128, c * 128:(c + 1) * 128]))

                for d2 in range(2):
                    @blk.tensor
                    def _(t, c=c, d2=d2):
                        p.io.wait(t)
                        p.v.wait(t)
                        for a in range(ND):
                            t.matmul(cb["p_a"][:, :],
                                     cb["atch"][:, a * 128:(a + 1) * 128],
                                     wouts[:, a * 1024 + d2 * 512:
                                           a * 1024 + (d2 + 1) * 512],
                                     start=(a == 0), stop=False)
                        p.t.inc(t.matmul(cb["p_a"][:, :], cb["ones_row"][0:1, 0:128],
                                         bos[0:1, dh * 1024 + d2 * 512:
                                             dh * 1024 + (d2 + 1) * 512],
                                         start=False, stop=True))

                    @blk.sync
                    def _(sy, c=c, d2=d2, dh=dh):
                        p.v.wait(sy)
                        p.io.dma(sy.dma_start(
                            out=x1s[:, :],
                            in_=x1[c * 128:(c + 1) * 128,
                                   dh * 1024 + d2 * 512: dh * 1024 + (d2 + 1) * 512]))

                    @blk.vector
                    def _(v, c=c, d2=d2):
                        p.t.wait(v)
                        p.io.wait(v)
                        p.v.inc(v.tensor_tensor(x2s[:, :], cb["p_a"][:, :], x1s[:, :],
                                                op=ALU.add))

                    @blk.sync
                    def _(sy, c=c, d2=d2, dh=dh):
                        p.v.wait(sy)
                        p.io.dma(sy.dma_start(
                            out=x2_out[c * 128:(c + 1) * 128,
                                       dh * 1024 + d2 * 512: dh * 1024 + (d2 + 1) * 512],
                            in_=x2s[:, :]))

        p.barrier()

        _emit_ln_pass(p, cb, x2_out, NT, 0, score=True)
        p.barrier()

        @blk.sync
        def _(sy):
            p.io.dma(sy.dma_start(out=sc3_out[:, :], in_=cb["sc_full"][:, 0:NT]))
        p.barrier()
    nc.finalize()
    return nc


def build_L3():
    nc = bacc.Bacc("TRN2", detect_race_conditions=False)
    x2 = nc.dram_tensor("x2", [T + 1, D], F32, kind="ExternalInput").ap()
    sc3 = nc.dram_tensor("sc3", [128, 2 * NT], F32, kind="ExternalInput").ap()
    w_r = nc.dram_tensor("w_r", [1, D], F32, kind="ExternalInput").ap()
    b_r = nc.dram_tensor("b_r", [1, 1], F32, kind="ExternalInput").ap()
    wrcol_d = nc.dram_tensor("wrcol_d", [128, ND], F32, kind="ExternalInput").ap()
    w1 = nc.dram_tensor("w1", [D, DFF], F32, kind="ExternalInput").ap()
    b1col = nc.dram_tensor("b1col", [128, NFF], F32, kind="ExternalInput").ap()
    w2 = nc.dram_tensor("w2", [DFF, D], F32, kind="ExternalInput").ap()
    b2col = nc.dram_tensor("b2col", [128, ND], F32, kind="ExternalInput").ap()

    lnx3 = nc.dram_tensor("lnx3", [T + 1, D], F32, kind="Internal").ap()
    yout = nc.dram_tensor("yout", [T + 1, D], F32, kind="ExternalOutput").ap()

    with ExitStack() as st, nc.Block() as blk:
        cb = _alloc_core(nc, st)
        p = Prog(st, nc, blk)
        _emit_common_consts(p, cb)
        _emit_score_consts(p, cb, w_r, b_r)
        p.barrier()

        @blk.sync
        def _(sy):
            p.io.dma(sy.dma_start(out=cb["sc_full"][:, :], in_=sc3[:, :]))
            p.io.dma(sy.dma_start(out=lnx3[T:T + 1, :], in_=cb["zrow"][0:1, :]))
            p.io.dma(sy.dma_start(out=yout[T:T + 1, :], in_=cb["zrow"][0:1, :]))

        _emit_ln_pass(p, cb, x2, NT, 0, lnx_dst=lnx3, base_dst=yout, score=False)
        p.barrier()
        _emit_ffn(p, cb, lnx3, yout, w1, b1col, w2, b2col, wrcol_d)
        p.barrier()
    nc.finalize()
    return nc


_NC_CACHE = {}


def _get(builder, key):
    if key not in _NC_CACHE:
        _NC_CACHE[key] = builder()
    return _NC_CACHE[key]


def kernel(L0_attn_x, L0_attn_w_qkv, L0_attn_b_qkv, L0_attn_w_out, L0_attn_b_out,
           L0_ffn_W_r, L0_ffn_b_r, L0_ffn_W1, L0_ffn_b1, L0_ffn_W2, L0_ffn_b2,
           L0_ffn_position_ids, L0_ffn_cache_position):
    x = np.ascontiguousarray(np.asarray(L0_attn_x, dtype=np.float32))
    wqkvT = np.ascontiguousarray(np.asarray(L0_attn_w_qkv, np.float32).T)
    b_qkv = np.asarray(L0_attn_b_qkv, np.float32)
    bq_col = np.ascontiguousarray(b_qkv[0:D].reshape(ND, 128).T)
    bkv_row = np.ascontiguousarray(b_qkv[D:3 * D].reshape(1, 2 * D))
    woutT = np.ascontiguousarray(np.asarray(L0_attn_w_out, np.float32).T)
    b_out = np.ascontiguousarray(np.asarray(L0_attn_b_out, np.float32).reshape(1, D))
    w_r = np.ascontiguousarray(np.asarray(L0_ffn_W_r, np.float32).reshape(1, D))
    wrcol = np.ascontiguousarray(np.asarray(L0_ffn_W_r, np.float32).reshape(ND, 128).T)
    b_r = np.ascontiguousarray(np.asarray(L0_ffn_b_r, np.float32).reshape(1, 1))
    w1 = np.ascontiguousarray(np.asarray(L0_ffn_W1, np.float32))
    b1col = np.ascontiguousarray(np.asarray(L0_ffn_b1, np.float32).reshape(NFF, 128).T)
    w2 = np.ascontiguousarray(np.asarray(L0_ffn_W2, np.float32))
    b2col = np.ascontiguousarray(np.asarray(L0_ffn_b2, np.float32).reshape(ND, 128).T)

    cores = list(range(8))

    nc1 = _get(build_L1, "L1")
    maps1 = []
    for c in cores:
        b, h = c // 2, c % 2
        maps1.append({
            "x0_my": np.ascontiguousarray(x[b, h * T:(h + 1) * T, :]),
            "x0_pr": np.ascontiguousarray(x[b, (1 - h) * T:(2 - h) * T, :]),
            "wqkvT": wqkvT, "bq_col": bq_col, "bkv_row": bkv_row,
            "w_r": w_r, "b_r": b_r, "wrcol_d": wrcol,
            "w1": w1, "b1col": b1col, "w2": w2, "b2col": b2col,
        })
    r1 = run_bass_kernel_spmd(nc1, maps1, core_ids=cores).results

    nc2 = _get(build_L2, "L2")
    maps2 = []
    zs16 = np.zeros((H * HD, HD), np.float32)
    for c in cores:
        b, h = c // 2, c % 2
        maps2.append({
            "x1": r1[c]["x1"],
            "attn_in": r1[c]["attn_out"],
            "q_in": r1[c]["q_out"],
            "s16_in": (np.ascontiguousarray(r1[2 * b]["s16_out"]) if h == 1 else zs16),
            "woutT": woutT, "b_out": b_out, "w_r": w_r, "b_r": b_r,
        })
    r2 = run_bass_kernel_spmd(nc2, maps2, core_ids=cores).results

    nc3 = _get(build_L3, "L3")
    maps3 = []
    for c in cores:
        b, h = c // 2, c % 2
        prt = 2 * b + (1 - h)
        sc3v = np.concatenate([r2[c]["sc3_out"], r2[prt]["sc3_out"]], axis=1)
        maps3.append({
            "x2": r2[c]["x2_out"],
            "sc3": np.ascontiguousarray(sc3v),
            "w_r": w_r, "b_r": b_r, "wrcol_d": wrcol,
            "w1": w1, "b1col": b1col, "w2": w2, "b2col": b2col,
        })
    r3 = run_bass_kernel_spmd(nc3, maps3, core_ids=cores).results

    out = np.empty((B, 2 * T, D), np.float32)
    for c in cores:
        b, h = c // 2, c % 2
        out[b, h * T:(h + 1) * T, :] = r3[c]["yout"][0:T, :]
    return out


# version 03
# speedup vs baseline: 1.2578x; 1.2578x over previous
import math
from contextlib import ExitStack

import numpy as np
import concourse.bacc as bacc
import concourse.mybir as mybir
from concourse.bass_utils import run_bass_kernel_spmd

F32 = mybir.dt.float32
I16 = mybir.dt.int16
U32 = mybir.dt.uint32
AF = mybir.ActivationFunctionType
ALU = mybir.AluOpType
AXX = mybir.AxisListType.X

B = 4
T = 2048
NT = T // 128
D = 2048
ND = D // 128
DFF = 8192
NFF = DFF // 128
H = 16
HD = 128
CAP = 384
NCAP = CAP // 128
EPS = 1e-5

_start = 2.0 ** (-2.0 ** (-(math.log2(H) - 3)))
SLOPES = [_start ** (i + 1) for i in range(H)]

ENGS = ("sync", "scalar", "vector", "tensor", "gpsimd")


class _Stop(Exception):
    pass


class Sem:
    def __init__(self, stack, nc, name):
        self.h = stack.enter_context(nc.semaphore(name))
        self.n = 0

    def inc(self, instr, k=1):
        instr.then_inc(self.h, k)
        self.n += k
        return instr

    def dma(self, instr):
        instr.then_inc(self.h, 16)
        self.n += 16
        return instr

    def wait(self, eng, n=None):
        eng.wait_ge(self.h, self.n if n is None else n)


class Prog:

    def vstep(self, eng, ins):
        self.v.inc(ins)
        self.v.wait(eng)

    def __init__(self, stack, nc, blk):
        self.nc = nc
        self.blk = blk
        self.io = Sem(stack, nc, "s_io")
        self.z = Sem(stack, nc, "s_z")
        self.gd = Sem(stack, nc, "s_gd")
        self.t = Sem(stack, nc, "s_t")
        self.v = Sem(stack, nc, "s_v")
        self.s = Sem(stack, nc, "s_s")
        self.g = Sem(stack, nc, "s_g")
        self.bar = {e: Sem(stack, nc, f"bar_{e}") for e in ENGS}

    def barrier(self):
        for e in ENGS:
            bs = self.bar[e]

            @getattr(self.blk, e)
            def _(eng, bs=bs):
                bs.inc(eng.nop())
        sems = [self.io, self.z, self.gd, self.t, self.v, self.s, self.g] + [self.bar[e] for e in ENGS]
        for e in ENGS:
            @getattr(self.blk, e)
            def _(eng):
                for sm in sems:
                    sm.wait(eng)


def _transpose_seq(p, cb, jobs, consume):
    done = []
    pp = (cb["p_tr"], cb["p_at"])
    for k, src in enumerate(jobs):
        pbuf = pp[k % 2]

        @p.blk.tensor
        def _(t, k=k, src=src, pbuf=pbuf):
            if k >= 2:
                sm, n = done[k - 2]
                sm.wait(t, n)
            p.t.inc(t.transpose(pbuf[:, 0:128], src, cb["ident"][:, :]))
        done.append(consume(k, pbuf))


def _stats_tile(p, cb, xt_ap, out_mvr):
    blk = p.blk

    @blk.vector
    def _(v):
        p.vstep(v, v.tensor_tensor(cb["lnt"][:, :], xt_ap[:, :], xt_ap[:, :],
                                   op=ALU.mult))
        v.reduce_sum(cb["tmp"][:, 4:5], xt_ap[:, :], axis=AXX)
        p.vstep(v, v.reduce_sum(cb["tmp"][:, 5:6], cb["lnt"][:, :], axis=AXX))
        v.tensor_scalar(cb["aggr"][:, 0:1], cb["tmp"][:, 4:5], 1.0 / D, None,
                        op0=ALU.mult)
        p.vstep(v, v.tensor_scalar(cb["tmp"][:, 6:7], cb["tmp"][:, 5:6], 1.0 / D,
                                   None, op0=ALU.mult))
        p.vstep(v, v.tensor_tensor(cb["tmp"][:, 7:8], cb["aggr"][:, 0:1],
                                   cb["aggr"][:, 0:1], op=ALU.mult))
        p.v.inc(v.tensor_sub(cb["aggr"][:, 1:2], cb["tmp"][:, 6:7], cb["tmp"][:, 7:8]))

    @blk.scalar
    def _(s):
        p.v.wait(s)
        p.s.inc(s.activation(cb["tmp"][:, 0:1], cb["aggr"][:, 1:2], AF.Sqrt,
                             bias=cb["epst"][:, 0:1], scale=1.0))

    @blk.vector
    def _(v):
        p.s.wait(v)
        p.vstep(v, v.reciprocal(out_mvr[:, 1:2], cb["tmp"][:, 0:1]))
        p.vstep(v, v.tensor_copy(out_mvr[:, 0:1], cb["aggr"][:, 0:1]))
        p.vstep(v, v.tensor_tensor(cb["tmp"][:, 1:2], out_mvr[:, 0:1],
                                   out_mvr[:, 1:2], op=ALU.mult))
        p.v.inc(v.tensor_scalar(out_mvr[:, 2:3], cb["tmp"][:, 1:2], -1.0, None,
                                op0=ALU.mult))


def _score_tile(p, cb, xt_ap, mvr, sc_dst):
    blk = p.blk

    @blk.vector
    def _(v):
        p.vstep(v, v.tensor_tensor(cb["lnt"][:, :], xt_ap[:, :], cb["wr_rep"][:, :],
                                   op=ALU.mult))
        v.tensor_tensor(cb["tmp"][:, 3:4], mvr[:, 0:1], cb["sumwr"][:, 0:1], op=ALU.mult)
        p.vstep(v, v.reduce_sum(cb["tmp"][:, 2:3], cb["lnt"][:, :], axis=AXX))
        p.vstep(v, v.tensor_sub(cb["tmp"][:, 2:3], cb["tmp"][:, 2:3], cb["tmp"][:, 3:4]))
        p.vstep(v, v.tensor_tensor(cb["tmp"][:, 2:3], cb["tmp"][:, 2:3], mvr[:, 1:2],
                                   op=ALU.mult))
        p.v.inc(v.tensor_tensor(sc_dst, cb["tmp"][:, 2:3], cb["br128"][:, 0:1],
                                op=ALU.add))


def _emit_score_consts(p, cb, w_r, b_r):
    blk = p.blk

    @blk.sync
    def _(sy):
        p.io.dma(sy.dma_start(out=cb["wr_rep"][:, :],
                              in_=w_r[0:1, :].to_broadcast([128, D])))
        p.io.dma(sy.dma_start(out=cb["brt"][0:1, 0:1], in_=b_r[0:1, 0:1]))

    @blk.vector
    def _(v):
        p.io.wait(v)
        p.v.inc(v.reduce_sum(cb["sumwr"][:, 0:1], cb["wr_rep"][:, :], axis=AXX))

    @blk.gpsimd
    def _(g):
        p.io.wait(g)
        p.g.inc(g.partition_broadcast(cb["br128"][:, 0:1], cb["brt"][0:1, 0:1],
                                      channels=128))


def _emit_ln_pass(p, cb, x_src, n_tiles, sc_col0, lnx_dst=None, base_dst=None,
                  score=True):
    blk = p.blk
    for t in range(n_tiles):
        @blk.sync
        def _(sy, t=t):
            p.v.wait(sy)
            p.s.wait(sy)
            p.io.wait(sy)
            p.io.dma(sy.dma_start(out=cb["xt"][:, :],
                                  in_=x_src[t * 128:(t + 1) * 128, :]))

        @blk.vector
        def _(v):
            p.io.wait(v)
        _stats_tile(p, cb, cb["xt"], cb["mvr"])
        if score:
            _score_tile(p, cb, cb["xt"], cb["mvr"],
                        cb["sc_full"][:, sc_col0 + t: sc_col0 + t + 1])
        if lnx_dst is not None:
            @blk.vector
            def _(v):
                p.vstep(v, v.scalar_tensor_tensor(cb["lnt"][:, :], cb["xt"][:, :],
                                                  cb["mvr"][:, 1:2],
                                                  cb["mvr"][:, 2:3].to_broadcast([128, D]),
                                                  op0=ALU.mult, op1=ALU.add))
                p.v.inc(v.scalar_tensor_tensor(cb["xt"][:, :], cb["lnt"][:, :], 0.5,
                                               cb["xt"][:, :], op0=ALU.mult, op1=ALU.add))

            @blk.sync
            def _(sy, t=t):
                p.v.wait(sy)
                p.io.dma(sy.dma_start(out=lnx_dst[t * 128:(t + 1) * 128, :],
                                      in_=cb["lnt"][:, :]))
                p.io.dma(sy.dma_start(out=base_dst[t * 128:(t + 1) * 128, :],
                                      in_=cb["xt"][:, :]))


def _emit_common_consts(p, cb):
    blk = p.blk

    @blk.gpsimd
    def _(g):
        g.iota(cb["i128a"][:, :], pattern=[[0, 128]], base=0, channel_multiplier=1,
               allow_small_or_imprecise_dtypes=True)
        g.iota(cb["row512"][0:1, :], pattern=[[1, 512]], base=0,
               channel_multiplier=0, allow_small_or_imprecise_dtypes=True)
        p.g.inc(g.iota(cb["i128b"][:, :], pattern=[[1, 128]], base=0,
                       channel_multiplier=0, allow_small_or_imprecise_dtypes=True))

    @blk.vector
    def _(v):
        p.g.wait(v)
        v.tensor_scalar(cb["ones_row"][:, :], cb["row512"][0:1, :], -1.0, None,
                        op0=ALU.is_ge)
        v.memset(cb["zrow"][:, :], 0.0)
        p.vstep(v, v.tensor_scalar(cb["epst"][:, :], cb["i128a"][:, 0:1], 0.0, EPS,
                                   op0=ALU.mult, op1=ALU.add))
        p.v.inc(v.tensor_tensor(cb["ident"][:, :], cb["i128a"][:, :], cb["i128b"][:, :],
                                op=ALU.is_equal))


def _emit_ffn(p, cb, lnx_dram, out_dram, w1, b1col, w2, b2col, wrcol_dram, phase=99):
    nc, blk = p.nc, p.blk

    @blk.gpsimd
    def _(g):
        p.v.wait(g)
        p.io.wait(g)
        p.g.inc(g.kth_largest(cb["thr2"][:, :], cb["sc_full"][:, 0:2 * NT], 2 * NT,
                              510, quantile=0.85))
        p.g.inc(g.iota(cb["iota16"][:, :], pattern=[[16, T // 16]], base=1,
                       channel_multiplier=1, allow_small_or_imprecise_dtypes=True))
        p.g.inc(g.partition_broadcast(cb["thr16"][:, :], cb["thr2"][0:1, 1:2],
                                      channels=16))

    @blk.sync
    def _(sy):
        p.v.wait(sy)
        p.io.wait(sy)
        for q in range(8):
            p.io.dma(sy.dma_start(out=cb["sc16"][:, q::8],
                                  in_=cb["sc_full"][16 * q:16 * (q + 1), 0:NT],
                                  allow_slow_non_contiguous=True))

    p.barrier()

    @blk.vector
    def _(v):
        v.memset(cb["cand"][:, T // 16:], float(T))
        v.memset(cb["idx16"][:, :], 0)
        p.vstep(v, v.tensor_scalar(cb["mask16"][:, :], cb["sc16"][:, :],
                                   cb["thr16"][:, 0:1], None, op0=ALU.is_gt))
        p.vstep(v, v.tensor_tensor(cb["cand"][:, 0:T // 16], cb["mask16"][:, :],
                                   cb["iota16"][:, :], op=ALU.mult))
        p.v.inc(v.tensor_scalar_add(cb["cand"][:, 0:T // 16],
                                    cb["cand"][:, 0:T // 16], -1.0))

    @blk.gpsimd
    def _(g):
        p.v.wait(g)
        p.g.inc(g.sparse_gather(cb["idxc"][:, :], cb["cand"][:, :],
                                num_found=cb["nf"][:, :]))

    @blk.vector
    def _(v):
        p.g.wait(v)
        p.v.inc(v.tensor_copy(cb["idx16"][0:16, :], cb["idxc"][:, 0:CAP // 16]))

    @blk.sync
    def _(sy):
        p.v.wait(sy)
        for k in range(1, 8):
            p.io.dma(sy.dma_start(out=cb["idx16"][16 * k:16 * (k + 1), :],
                                  in_=cb["idx16"][0:16, :]))

    p.barrier()
    if phase < 3:
        return

    with ExitStack() as ost:
        osb = lambda name, shape: ost.enter_context(nc.sbuf_tensor(name, shape, F32))
        xgT = osb("xgT", [128, ND * CAP])
        wrcol = osb("wrcol", [128, ND])
        b1t = osb("b1t", [128, NFF])
        b2t = osb("b2t", [128, ND])
        wg05 = osb("wg05", [128, NCAP])

        @blk.sync
        def _(sy):
            p.io.dma(sy.dma_start(out=wrcol[:, :], in_=wrcol_dram[:, :]))
            p.io.dma(sy.dma_start(out=b1t[:, :], in_=b1col[:, :]))
            p.io.dma(sy.dma_start(out=b2t[:, :], in_=b2col[:, :]))

        with ExitStack() as gst:
            xg = gst.enter_context(nc.sbuf_tensor("xg", [128, NCAP, D], F32))

            @blk.gpsimd
            def _(g):
                p.io.wait(g)
                p.gd.dma(g.dma_gather(xg[:, :, :], lnx_dram[:, :], cb["idx16"][:, :],
                                      CAP, CAP, D))

            p.barrier()
            if phase == 31:
                raise _Stop

            jobs = [xg[:, s, i * 128:(i + 1) * 128]
                    for s in range(NCAP) for i in range(ND)]

            def consume(k, pbuf):
                s4, i = divmod(k, ND)

                @blk.vector
                def _(v, s4=s4, i=i, pbuf=pbuf):
                    p.t.wait(v)
                    p.v.inc(v.tensor_copy(
                        xgT[:, i * CAP + s4 * 128: i * CAP + (s4 + 1) * 128],
                        pbuf[:, 0:128]))
                return (p.v, p.v.n)

            _transpose_seq(p, cb, jobs, consume)
            p.barrier()
            if phase == 32:
                raise _Stop

            @blk.vector
            def _(v):
                for s4 in range(NCAP):
                    p.v.inc(v.tensor_scalar(xg[:, s4, :], xg[:, s4, :], -0.5, None,
                                            op0=ALU.mult))

            p.barrier()

            @blk.gpsimd
            def _(g):
                p.gd.dma(g.dma_scatter_add(out_dram[:, :], xg[:, :, :],
                                           cb["idx16"][:, :], CAP, CAP, D))

            p.barrier()
            if phase == 33:
                raise _Stop

        for s4 in range(NCAP):
            @blk.tensor
            def _(t, s4=s4):
                if s4 > 0:
                    p.v.wait(t)
                for i in range(ND):
                    ins = t.matmul(cb["p_tr"][:, 0:1],
                                   xgT[:, i * CAP + s4 * 128: i * CAP + (s4 + 1) * 128],
                                   wrcol[:, i:i + 1],
                                   start=(i == 0), stop=(i == ND - 1))
                p.t.inc(ins)

            @blk.vector
            def _(v, s4=s4):
                p.t.wait(v)
                p.v.inc(v.tensor_scalar(wg05[:, s4:s4 + 1], cb["p_tr"][:, 0:1],
                                        cb["br128"][:, 0:1], 0.5,
                                        op0=ALU.add, op1=ALU.mult))

        p.barrier()
        if phase == 34:
            raise _Stop
        if phase < 4:
            return

        with ExitStack() as hst:
            hbuf = hst.enter_context(nc.sbuf_tensor("hbuf", [128, NFF * CAP], F32))

            with ExitStack() as wst:
                w1buf = wst.enter_context(nc.sbuf_tensor("w1buf", [128, 2, ND, 128], F32))
                w1r = w1.rearrange("(i p) f -> p i f", p=128)
                tcnt = {}
                scnt = {}
                zcnt = {}
                for j in range(NFF):
                    @blk.sync
                    def _(sy, j=j):
                        if j >= 2:
                            p.t.wait(sy, tcnt[j - 2])
                        p.z.dma(sy.dma_start(out=w1buf[:, j % 2, :, :],
                                             in_=w1r[:, :, j * 128:(j + 1) * 128]))
                        zcnt[j] = p.z.n

                    @blk.tensor
                    def _(t, j=j):
                        p.z.wait(t, zcnt[j])
                        if j >= 1:
                            p.s.wait(t, scnt[j - 1])
                        for i in range(ND):
                            ins = t.matmul(cb["p_h"][:, 0:CAP],
                                           w1buf[:, j % 2, i, :],
                                           xgT[:, i * CAP:(i + 1) * CAP],
                                           start=(i == 0), stop=(i == ND - 1))
                        p.t.inc(ins)
                        tcnt[j] = p.t.n

                    @blk.scalar
                    def _(s, j=j):
                        p.t.wait(s, tcnt[j])
                        p.s.inc(s.activation(hbuf[:, j * CAP:(j + 1) * CAP],
                                             cb["p_h"][:, 0:CAP], AF.Gelu_apprx_tanh,
                                             bias=b1t[:, j:j + 1], scale=1.0))
                        scnt[j] = p.s.n
                p.barrier()
            if phase < 5:
                return

            with ExitStack() as wst2:
                w2buf = wst2.enter_context(nc.sbuf_tensor("w2buf", [128, 2, 6 * 128], F32))
                blkF = wst2.enter_context(nc.sbuf_tensor("blkF", [128, 6 * CAP], F32))
                vtile = wst2.enter_context(nc.sbuf_tensor("vtile", [128, NCAP, D], F32))
                w2r = w2.rearrange("(j p) d -> p j d", p=128)
                pbanks = [cb["p_a"], cb["p_b"], cb["p_c"], cb["p_d"], cb["p_h"],
                          cb["p_sc"]]
                dgroups = [list(range(0, 6)), list(range(6, 12)), list(range(12, 16))]
                for dgrp in dgroups:
                    w2cnt = {}
                    zcnt2 = {}
                    for j in range(NFF):
                        @blk.sync
                        def _(sy, j=j, dgrp=dgrp):
                            if j >= 2:
                                p.t.wait(sy, w2cnt[j - 2])
                            p.z.dma(sy.dma_start(
                                out=w2buf[:, j % 2, 0: len(dgrp) * 128],
                                in_=w2r[:, j, dgrp[0] * 128:(dgrp[-1] + 1) * 128]))
                            zcnt2[j] = p.z.n

                        @blk.tensor
                        def _(t, j=j, dgrp=dgrp):
                            p.z.wait(t, zcnt2[j])
                            for di in range(len(dgrp)):
                                ins = t.matmul(pbanks[di][:, 0:CAP],
                                               w2buf[:, j % 2, di * 128:(di + 1) * 128],
                                               hbuf[:, j * CAP:(j + 1) * CAP],
                                               start=(j == 0), stop=(j == NFF - 1))
                            p.t.inc(ins)
                            w2cnt[j] = p.t.n

                    @blk.vector
                    def _(v, dgrp=dgrp):
                        p.t.wait(v)
                        for di, dc in enumerate(dgrp):
                            p.v.inc(v.tensor_scalar(blkF[:, di * CAP:(di + 1) * CAP],
                                                    pbanks[di][:, 0:CAP],
                                                    b2t[:, dc:dc + 1], None,
                                                    op0=ALU.add))

                    p.barrier()

                    jobs = [blkF[:, di * CAP + s4 * 128: di * CAP + (s4 + 1) * 128]
                            for di in range(len(dgrp)) for s4 in range(NCAP)]

                    def consume(k, pbuf, dgrp=dgrp):
                        di, s4 = divmod(k, NCAP)
                        dc = dgrp[di]

                        @blk.vector
                        def _(v, s4=s4, dc=dc, pbuf=pbuf):
                            p.t.wait(v)
                            p.v.inc(v.tensor_scalar(
                                vtile[:, s4, dc * 128:(dc + 1) * 128],
                                pbuf[:, 0:128], wg05[:, s4:s4 + 1], None,
                                op0=ALU.mult))
                        return (p.v, p.v.n)

                    _transpose_seq(p, cb, jobs, consume)
                    p.barrier()

                @blk.gpsimd
                def _(g):
                    p.gd.dma(g.dma_scatter_add(out_dram[:, :], vtile[:, :, :],
                                               cb["idx16"][:, :], CAP, CAP, D))
                p.barrier()


def _alloc_core(nc, st):
    sb = lambda name, shape, dt=F32: st.enter_context(nc.sbuf_tensor(name, shape, dt))
    pst = lambda name, shape: st.enter_context(nc.psum_tensor(name, shape, F32))
    cb = {}
    cb["xt"] = sb("xt", [128, D])
    cb["lnt"] = sb("lnt", [128, D])
    cb["stats"] = sb("stats", [128, 24])
    cb["aggr"] = sb("aggr", [128, 2])
    cb["mvr"] = sb("mvr", [128, 3])
    cb["tmp"] = sb("tmp", [128, 8])
    cb["wr_rep"] = sb("wr_rep", [128, D])
    cb["sumwr"] = sb("sumwr", [128, 1])
    cb["brt"] = sb("brt", [1, 1])
    cb["br128"] = sb("br128", [128, 1])
    cb["ones_row"] = sb("ones_row", [1, 512])
    cb["zrow"] = sb("zrow", [1, D])
    cb["ident"] = sb("ident", [128, 128])
    cb["epst"] = sb("epst", [128, 1])
    cb["row512"] = sb("row512", [1, 512])
    cb["i128a"] = sb("i128a", [128, 128])
    cb["i128b"] = sb("i128b", [128, 128])
    cb["sc_full"] = sb("sc_full", [128, 2 * NT])
    cb["sc16"] = sb("sc16", [16, T // 16])
    cb["thr2"] = sb("thr2", [128, 2])
    cb["thr16"] = sb("thr16", [16, 1])
    cb["iota16"] = sb("iota16", [16, T // 16])
    cb["cand"] = sb("cand", [16, (T + CAP) // 16])
    cb["mask16"] = sb("mask16", [16, T // 16])
    cb["idxc"] = sb("idxc", [16, 56])
    cb["nf"] = sb("nf", [1, 1], U32)
    cb["idx16"] = sb("idx16", [128, CAP // 16], I16)
    cb["p_a"] = pst("p_a", [128, 512])
    cb["p_b"] = pst("p_b", [128, 512])
    cb["p_c"] = pst("p_c", [128, 512])
    cb["p_d"] = pst("p_d", [128, 512])
    cb["p_h"] = pst("p_h", [128, 512])
    cb["p_sc"] = pst("p_sc", [128, 512])
    cb["p_tr"] = pst("p_tr", [128, 128])
    cb["p_at"] = pst("p_at", [128, 128])
    return cb


def _alloc_attn(nc, st, cb, full=True):
    sb = lambda name, shape: st.enter_context(nc.sbuf_tensor(name, shape, F32))
    cb["dq_base"] = sb("dq_base", [128, 128])
    cb["c128"] = sb("c128", [128, 1])
    cb["dq_b"] = sb("dq_b", [128, H * 128])
    cb["lamc_c"] = sb("lamc_c", [128, H])
    cb["Sst"] = sb("Sst", [128, H * HD])
    cb["qch"] = sb("qch", [128, H * 128])
    cb["atch"] = sb("atch", [128, H * 128])
    cb["qdT"] = sb("qdT", [128, 128])
    if full:
        cb["diffT"] = sb("diffT", [128, 128])
        cb["causalT"] = sb("causalT", [128, 128])
        cb["dk_base"] = sb("dk_base", [128, 1])
        cb["dmaskT"] = sb("dmaskT", [128, H * 128])
        cb["dk_c"] = sb("dk_c", [128, H])
        cb["kch"] = sb("kch", [128, D])
        cb["vch"] = sb("vch", [128, D])
        cb["kTh"] = sb("kTh", [128, 128])
        cb["scT"] = sb("scT", [128, 128])
        cb["kdh"] = sb("kdh", [128, 128])


def _emit_attn_consts(p, cb, full=True):
    blk = p.blk

    @blk.gpsimd
    def _(g):
        if full:
            g.iota(cb["diffT"][:, :], pattern=[[1, 128]], base=0, channel_multiplier=-1,
                   allow_small_or_imprecise_dtypes=True)
            g.iota(cb["dk_base"][:, 0:1], pattern=[[0, 1]], base=127,
                   channel_multiplier=-1, allow_small_or_imprecise_dtypes=True)
        p.g.inc(g.iota(cb["dq_base"][:, :], pattern=[[1, 128]], base=1,
                       channel_multiplier=0, allow_small_or_imprecise_dtypes=True))

    @blk.vector
    def _(v):
        p.g.wait(v)
        p.v.inc(v.tensor_scalar(cb["c128"][:, 0:1], cb["dq_base"][:, 0:1], 0.0, 128.0,
                                op0=ALU.mult, op1=ALU.add))

    if full:
        @blk.vector
        def _(v):
            p.g.wait(v)
            p.vstep(v, v.tensor_scalar(cb["causalT"][:, :], cb["diffT"][:, :], 0.0,
                                       None, op0=ALU.is_ge))
            p.v.inc(v.tensor_scalar(cb["diffT"][:, :], cb["diffT"][:, :], 0.0, None,
                                    op0=ALU.max))

    @blk.scalar
    def _(s):
        p.g.wait(s)
        p.v.wait(s)
        for hh in range(H):
            sl = SLOPES[hh]
            if full:
                s.activation(cb["dmaskT"][:, hh * 128:(hh + 1) * 128], cb["diffT"][:, :],
                             AF.Exp, scale=-sl)
                s.activation(cb["dk_c"][:, hh:hh + 1], cb["dk_base"][:, 0:1],
                             AF.Exp, scale=-sl)
            s.activation(cb["dq_b"][:, hh * 128:(hh + 1) * 128], cb["dq_base"][:, :],
                         AF.Exp, scale=-sl)
            ins = s.activation(cb["lamc_c"][:, hh:hh + 1], cb["c128"][:, 0:1],
                               AF.Exp, scale=-sl)
        p.s.inc(ins)

    if full:
        @blk.vector
        def _(v):
            p.s.wait(v)
            for hh in range(H):
                ins = v.tensor_tensor(cb["dmaskT"][:, hh * 128:(hh + 1) * 128],
                                      cb["dmaskT"][:, hh * 128:(hh + 1) * 128],
                                      cb["causalT"][:, :], op=ALU.mult)
            p.v.inc(ins)


def build_L1(phase=99):
    nc = bacc.Bacc("TRN2", detect_race_conditions=False)
    x0_my = nc.dram_tensor("x0_my", [T, D], F32, kind="ExternalInput").ap()
    x0_pr = nc.dram_tensor("x0_pr", [T, D], F32, kind="ExternalInput").ap()
    wqkvT = nc.dram_tensor("wqkvT", [D, 3 * D], F32, kind="ExternalInput").ap()
    bq_col = nc.dram_tensor("bq_col", [128, ND], F32, kind="ExternalInput").ap()
    bkv_row = nc.dram_tensor("bkv_row", [1, 2 * D], F32, kind="ExternalInput").ap()
    w_r = nc.dram_tensor("w_r", [1, D], F32, kind="ExternalInput").ap()
    b_r = nc.dram_tensor("b_r", [1, 1], F32, kind="ExternalInput").ap()
    wrcol_d = nc.dram_tensor("wrcol_d", [128, ND], F32, kind="ExternalInput").ap()
    w1 = nc.dram_tensor("w1", [D, DFF], F32, kind="ExternalInput").ap()
    b1col = nc.dram_tensor("b1col", [128, NFF], F32, kind="ExternalInput").ap()
    w2 = nc.dram_tensor("w2", [DFF, D], F32, kind="ExternalInput").ap()
    b2col = nc.dram_tensor("b2col", [128, ND], F32, kind="ExternalInput").ap()

    lnx1 = nc.dram_tensor("lnx1", [T + 1, D], F32, kind="Internal").ap()
    x1 = nc.dram_tensor("x1", [T + 1, D], F32, kind="ExternalOutput").ap()
    q_out = nc.dram_tensor("q_out", [D, T], F32, kind="ExternalOutput").ap()
    k_dram = nc.dram_tensor("k_dram", [T, D], F32, kind="Internal").ap()
    v_dram = nc.dram_tensor("v_dram", [T, D], F32, kind="Internal").ap()
    attn_out = nc.dram_tensor("attn_out", [D, T], F32, kind="ExternalOutput").ap()
    s16_out = nc.dram_tensor("s16_out", [H * HD, HD], F32, kind="ExternalOutput").ap()
    sc_out = nc.dram_tensor("sc_out", [128, NT], F32, kind="ExternalOutput").ap()

    try:
      with ExitStack() as st, nc.Block() as blk:
        cb = _alloc_core(nc, st)
        p = Prog(st, nc, blk)
        _emit_common_consts(p, cb)
        _emit_score_consts(p, cb, w_r, b_r)
        p.barrier()

        @blk.sync
        def _(sy):
            p.io.dma(sy.dma_start(out=lnx1[T:T + 1, :], in_=cb["zrow"][0:1, :]))
            p.io.dma(sy.dma_start(out=x1[T:T + 1, :], in_=cb["zrow"][0:1, :]))

        _emit_ln_pass(p, cb, x0_my, NT, 0, lnx_dst=lnx1, base_dst=x1, score=True)
        _emit_ln_pass(p, cb, x0_pr, NT, NT, score=True)
        p.barrier()
        if phase < 2:
            raise _Stop

        @blk.sync
        def _(sy):
            p.io.dma(sy.dma_start(out=sc_out[:, :], in_=cb["sc_full"][:, 0:NT]))

        _emit_ffn(p, cb, lnx1, x1, w1, b1col, w2, b2col, wrcol_d, phase=phase)
        if phase < 6:
            raise _Stop

        with ExitStack() as qst:
            qal = lambda name, shape: qst.enter_context(nc.sbuf_tensor(name, shape, F32))
            lnxT = qal("lnxT", [128, ND * 1024])
            wstrip = qal("wstrip", [128, 2, ND, 512])
            qwbuf = qal("qwbuf", [128, 2, ND, 128])
            bqs = qal("bqs", [128, ND])
            bkvs = qal("bkvs", [1, 2 * D])
            osb = qal("osb", [128, 512])

            @blk.sync
            def _(sy):
                p.io.dma(sy.dma_start(out=bqs[:, :], in_=bq_col[:, :]))
                p.io.dma(sy.dma_start(out=bkvs[0:1, :], in_=bkv_row[0:1, :]))

            for th in range(2):
                for tl in range(8):
                    t_g = th * 8 + tl

                    @blk.sync
                    def _(sy, t_g=t_g):
                        p.v.wait(sy)
                        p.t.wait(sy)
                        p.io.wait(sy)
                        p.io.dma(sy.dma_start(out=cb["xt"][:, :],
                                              in_=x1[t_g * 128:(t_g + 1) * 128, :]))

                    @blk.vector
                    def _(v):
                        p.io.wait(v)
                    _stats_tile(p, cb, cb["xt"], cb["mvr"])

                    @blk.vector
                    def _(v):
                        p.v.inc(v.scalar_tensor_tensor(
                            cb["lnt"][:, :], cb["xt"][:, :], cb["mvr"][:, 1:2],
                            cb["mvr"][:, 2:3].to_broadcast([128, D]),
                            op0=ALU.mult, op1=ALU.add))

                    jobs = [cb["lnt"][:, i * 128:(i + 1) * 128] for i in range(ND)]

                    def consume(k, pbuf, tl=tl):
                        @blk.vector
                        def _(v, k=k, pbuf=pbuf, tl=tl):
                            p.t.wait(v)
                            p.v.inc(v.tensor_copy(
                                lnxT[:, k * 1024 + tl * 128: k * 1024 + (tl + 1) * 128],
                                pbuf[:, 0:128]))
                        return (p.v, p.v.n)

                    @blk.tensor
                    def _(t):
                        p.v.wait(t)
                    _transpose_seq(p, cb, jobs, consume)

                p.barrier()

                wqr = wqkvT.rearrange("(i p) f -> p i f", p=128)
                qtc = {}
                for ofc in range(ND):
                    @blk.sync
                    def _(sy, ofc=ofc):
                        if ofc >= 2:
                            p.t.wait(sy, qtc[ofc - 2])
                        p.z.dma(sy.dma_start(
                            out=qwbuf[:, ofc % 2, :, :],
                            in_=wqr[:, :, ofc * 128:(ofc + 1) * 128]))
                        qtc[(ofc, "z")] = p.z.n

                    for g2 in range(2):
                        @blk.tensor
                        def _(t, ofc=ofc, g2=g2):
                            p.z.wait(t, qtc[(ofc, "z")])
                            p.s.wait(t)
                            for dc in range(ND):
                                ins = t.matmul(cb["p_a"][:, :],
                                               qwbuf[:, ofc % 2, dc, :],
                                               lnxT[:, dc * 1024 + g2 * 512:
                                                    dc * 1024 + (g2 + 1) * 512],
                                               start=(dc == 0), stop=(dc == ND - 1))
                            p.t.inc(ins)
                            qtc[ofc] = p.t.n

                        @blk.vector
                        def _(v, ofc=ofc, g2=g2):
                            p.t.wait(v)
                            p.io.wait(v)
                            p.s.inc(v.tensor_scalar(osb[:, :], cb["p_a"][:, :],
                                                    bqs[:, ofc:ofc + 1], None,
                                                    op0=ALU.add))

                        @blk.sync
                        def _(sy, ofc=ofc, g2=g2, th=th):
                            p.s.wait(sy)
                            p.io.dma(sy.dma_start(
                                out=q_out[ofc * 128:(ofc + 1) * 128,
                                          th * 1024 + g2 * 512:
                                          th * 1024 + (g2 + 1) * 512],
                                in_=osb[:, :]))

                kvc = {}
                for sp in range(8):
                    @blk.sync
                    def _(sy, sp=sp):
                        if sp >= 2:
                            p.t.wait(sy, kvc[sp - 2])
                        p.z.dma(sy.dma_start(
                            out=wstrip[:, sp % 2, :, :],
                            in_=wqr[:, :, D + sp * 512: D + (sp + 1) * 512]))
                        kvc[(sp, "z")] = p.z.n

                    for tl in range(8):
                        @blk.tensor
                        def _(t, sp=sp, tl=tl):
                            p.z.wait(t, kvc[(sp, "z")])
                            p.s.wait(t)
                            for dc in range(ND):
                                t.matmul(cb["p_b"][:, :],
                                         lnxT[:, dc * 1024 + tl * 128:
                                              dc * 1024 + (tl + 1) * 128],
                                         wstrip[:, sp % 2, dc, :],
                                         start=(dc == 0), stop=False)
                            p.t.inc(t.matmul(cb["p_b"][:, :], cb["ones_row"][0:1, 0:128],
                                             bkvs[0:1, sp * 512:(sp + 1) * 512],
                                             start=False, stop=True))
                            kvc[sp] = p.t.n

                        @blk.scalar
                        def _(s, sp=sp, tl=tl):
                            p.t.wait(s)
                            p.io.wait(s)
                            p.s.inc(s.copy(osb[:, :], cb["p_b"][:, :]))

                        @blk.sync
                        def _(sy, sp=sp, tl=tl, th=th):
                            p.s.wait(sy)
                            row = (th * 8 + tl) * 128
                            dst = k_dram if sp < 4 else v_dram
                            col = (sp % 4) * 512
                            p.io.dma(sy.dma_start(out=dst[row:row + 128, col:col + 512],
                                                  in_=osb[:, :]))

                p.barrier()
        if phase < 7:
            raise _Stop

        with ExitStack() as ast:
            _alloc_attn(nc, ast, cb, full=True)
            _emit_attn_consts(p, cb, full=True)

            @blk.vector
            def _(v):
                p.v.inc(v.tensor_scalar(cb["Sst"][:, :], cb["xt"][:, :], 0.0, None,
                                        op0=ALU.mult))
            p.barrier()

            for c in range(NT):
                @blk.sync
                def _(sy, c=c):
                    p.v.wait(sy)
                    p.t.wait(sy)
                    p.io.dma(sy.dma_start(out=cb["kch"][:, :],
                                          in_=k_dram[c * 128:(c + 1) * 128, :]))
                    p.io.dma(sy.dma_start(out=cb["vch"][:, :],
                                          in_=v_dram[c * 128:(c + 1) * 128, :]))
                    for hh in range(H):
                        p.io.dma(sy.dma_start(
                            out=cb["qch"][:, hh * 128:(hh + 1) * 128],
                            in_=q_out[hh * 128:(hh + 1) * 128, c * 128:(c + 1) * 128]))

                for hh in range(H):
                    hs = slice(hh * 128, (hh + 1) * 128)

                    @blk.tensor
                    def _(t, hs=hs):
                        p.io.wait(t)
                        p.v.wait(t)
                        p.t.inc(t.transpose(cb["p_tr"][:, 0:128], cb["kch"][:, hs],
                                            cb["ident"][:, :]))

                    @blk.vector
                    def _(v, hs=hs):
                        p.t.wait(v)
                        v.tensor_copy(cb["kTh"][:, :], cb["p_tr"][:, 0:128])
                        p.v.inc(v.tensor_tensor(cb["qdT"][:, :], cb["qch"][:, hs],
                                                cb["dq_b"][:, hs], op=ALU.mult))

                    @blk.vector
                    def _(v, hh=hh, hs=hs):
                        p.s.inc(v.tensor_scalar(cb["kdh"][:, :], cb["kch"][:, hs],
                                                cb["dk_c"][:, hh:hh + 1], None,
                                                op0=ALU.mult))

                    @blk.tensor
                    def _(t, hs=hs):
                        p.v.wait(t)
                        p.s.wait(t)
                        p.t.inc(t.matmul(cb["p_at"][:, 0:128], cb["kTh"][:, :],
                                         cb["qch"][:, hs], start=True, stop=True))

                    @blk.vector
                    def _(v, hs=hs):
                        p.t.wait(v)
                        p.v.inc(v.tensor_tensor(cb["scT"][:, :], cb["p_at"][:, 0:128],
                                                cb["dmaskT"][:, hs], op=ALU.mult))

                    @blk.tensor
                    def _(t, hs=hs):
                        p.v.wait(t)
                        t.matmul(cb["p_tr"][:, 0:128], cb["Sst"][:, hs], cb["qdT"][:, :],
                                 start=True, stop=False)
                        t.matmul(cb["p_tr"][:, 0:128], cb["vch"][:, hs], cb["scT"][:, :],
                                 start=False, stop=True)
                        p.t.inc(t.matmul(cb["p_at"][:, 0:128], cb["kdh"][:, :],
                                         cb["vch"][:, hs], start=True, stop=True))

                    @blk.vector
                    def _(v, hs=hs, hh=hh):
                        p.t.wait(v)
                        v.tensor_copy(cb["atch"][:, hs], cb["p_tr"][:, 0:128])
                        p.v.inc(v.scalar_tensor_tensor(cb["Sst"][:, hs], cb["Sst"][:, hs],
                                                       cb["lamc_c"][:, hh:hh + 1],
                                                       cb["p_at"][:, 0:128],
                                                       op0=ALU.mult, op1=ALU.add))

                @blk.sync
                def _(sy, c=c):
                    p.v.wait(sy)
                    for hh in range(H):
                        p.io.dma(sy.dma_start(
                            out=attn_out[hh * 128:(hh + 1) * 128, c * 128:(c + 1) * 128],
                            in_=cb["atch"][:, hh * 128:(hh + 1) * 128]))

            p.barrier()

            @blk.sync
            def _(sy):
                for hh in range(H):
                    p.io.dma(sy.dma_start(out=s16_out[hh * 128:(hh + 1) * 128, :],
                                          in_=cb["Sst"][:, hh * HD:(hh + 1) * HD]))

        p.barrier()
    except _Stop:
        pass
    nc.finalize()
    return nc


def build_L2():
    nc = bacc.Bacc("TRN2", detect_race_conditions=False)
    x1 = nc.dram_tensor("x1", [T + 1, D], F32, kind="ExternalInput").ap()
    attn_in = nc.dram_tensor("attn_in", [D, T], F32, kind="ExternalInput").ap()
    q_in = nc.dram_tensor("q_in", [D, T], F32, kind="ExternalInput").ap()
    s16_in = nc.dram_tensor("s16_in", [H * HD, HD], F32, kind="ExternalInput").ap()
    woutT = nc.dram_tensor("woutT", [D, D], F32, kind="ExternalInput").ap()
    b_out = nc.dram_tensor("b_out", [1, D], F32, kind="ExternalInput").ap()
    w_r = nc.dram_tensor("w_r", [1, D], F32, kind="ExternalInput").ap()
    b_r = nc.dram_tensor("b_r", [1, 1], F32, kind="ExternalInput").ap()

    x2_out = nc.dram_tensor("x2_out", [T + 1, D], F32, kind="ExternalOutput").ap()
    sc3_out = nc.dram_tensor("sc3_out", [128, NT], F32, kind="ExternalOutput").ap()

    with ExitStack() as st, nc.Block() as blk:
        cb = _alloc_core(nc, st)
        p = Prog(st, nc, blk)
        _emit_common_consts(p, cb)
        _emit_score_consts(p, cb, w_r, b_r)
        _alloc_attn(nc, st, cb, full=False)
        _emit_attn_consts(p, cb, full=False)
        sb = lambda name, shape: st.enter_context(nc.sbuf_tensor(name, shape, F32))
        wouts = sb("wouts", [128, ND * 1024])
        lamcp = sb("lamcp", [128, H * NT])
        c128g = sb("c128g", [128, NT])
        bos = sb("bos", [1, D])
        x1s = sb("x1s", [128, 512])
        x2s = sb("x2s", [128, 512])
        at2 = sb("at2", [128, H * 128])

        @blk.gpsimd
        def _(g):
            p.g.inc(g.iota(c128g[:, :], pattern=[[128, NT]], base=0,
                           channel_multiplier=0, allow_small_or_imprecise_dtypes=True))

        @blk.scalar
        def _(s):
            p.g.wait(s)
            for hh in range(H):
                ins = s.activation(lamcp[:, hh * NT:(hh + 1) * NT], c128g[:, :],
                                   AF.Exp, scale=-SLOPES[hh])
            p.s.inc(ins)

        @blk.sync
        def _(sy):
            p.io.dma(sy.dma_start(out=bos[0:1, :], in_=b_out[0:1, :]))
            for hh in range(H):
                p.io.dma(sy.dma_start(out=cb["Sst"][:, hh * HD:(hh + 1) * HD],
                                      in_=s16_in[hh * 128:(hh + 1) * 128, :]))
            p.io.dma(sy.dma_start(out=x2_out[T:T + 1, :], in_=cb["zrow"][0:1, :]))

        p.barrier()

        attn2 = nc.dram_tensor("attn2", [D, T], F32, kind="Internal").ap()

        for c in range(NT):
            @blk.sync
            def _(sy, c=c):
                p.v.wait(sy)
                p.io.wait(sy)
                for hh in range(H):
                    p.io.dma(sy.dma_start(
                        out=cb["qch"][:, hh * 128:(hh + 1) * 128],
                        in_=q_in[hh * 128:(hh + 1) * 128, c * 128:(c + 1) * 128]))
                    p.io.dma(sy.dma_start(
                        out=cb["atch"][:, hh * 128:(hh + 1) * 128],
                        in_=attn_in[hh * 128:(hh + 1) * 128, c * 128:(c + 1) * 128]))

            for hh in range(H):
                hs = slice(hh * 128, (hh + 1) * 128)

                @blk.vector
                def _(v, hs=hs):
                    p.io.wait(v)
                    p.t.wait(v)
                    p.v.inc(v.tensor_tensor(cb["qdT"][:, :], cb["qch"][:, hs],
                                            cb["dq_b"][:, hs], op=ALU.mult))

                @blk.tensor
                def _(t, hs=hs):
                    p.v.wait(t)
                    p.t.inc(t.matmul(cb["p_at"][:, 0:128], cb["Sst"][:, hs],
                                     cb["qdT"][:, :], start=True, stop=True))

                @blk.vector
                def _(v, c=c, hh=hh, hs=hs):
                    p.t.wait(v)
                    p.v.inc(v.scalar_tensor_tensor(
                        at2[:, hs], cb["p_at"][:, 0:128],
                        lamcp[:, hh * NT + c: hh * NT + c + 1],
                        cb["atch"][:, hs], op0=ALU.mult, op1=ALU.add))

            @blk.sync
            def _(sy, c=c):
                p.v.wait(sy)
                for hh in range(H):
                    p.io.dma(sy.dma_start(
                        out=attn2[hh * 128:(hh + 1) * 128, c * 128:(c + 1) * 128],
                        in_=at2[:, hh * 128:(hh + 1) * 128]))

        p.barrier()

        for dh in range(2):
            @blk.sync
            def _(sy, dh=dh):
                p.t.wait(sy)
                for dc in range(ND):
                    p.io.dma(sy.dma_start(
                        out=wouts[:, dc * 1024:(dc + 1) * 1024],
                        in_=woutT[dc * 128:(dc + 1) * 128,
                                  dh * 1024:(dh + 1) * 1024]))
            p.barrier()

            for c in range(NT):
                @blk.sync
                def _(sy, c=c):
                    p.v.wait(sy)
                    for hh in range(H):
                        p.io.dma(sy.dma_start(
                            out=cb["atch"][:, hh * 128:(hh + 1) * 128],
                            in_=attn2[hh * 128:(hh + 1) * 128, c * 128:(c + 1) * 128]))

                for d2 in range(2):
                    @blk.tensor
                    def _(t, c=c, d2=d2):
                        p.io.wait(t)
                        p.v.wait(t)
                        for a in range(ND):
                            t.matmul(cb["p_a"][:, :],
                                     cb["atch"][:, a * 128:(a + 1) * 128],
                                     wouts[:, a * 1024 + d2 * 512:
                                           a * 1024 + (d2 + 1) * 512],
                                     start=(a == 0), stop=False)
                        p.t.inc(t.matmul(cb["p_a"][:, :], cb["ones_row"][0:1, 0:128],
                                         bos[0:1, dh * 1024 + d2 * 512:
                                             dh * 1024 + (d2 + 1) * 512],
                                         start=False, stop=True))

                    @blk.sync
                    def _(sy, c=c, d2=d2, dh=dh):
                        p.v.wait(sy)
                        p.io.dma(sy.dma_start(
                            out=x1s[:, :],
                            in_=x1[c * 128:(c + 1) * 128,
                                   dh * 1024 + d2 * 512: dh * 1024 + (d2 + 1) * 512]))

                    @blk.vector
                    def _(v, c=c, d2=d2):
                        p.t.wait(v)
                        p.io.wait(v)
                        p.v.inc(v.tensor_tensor(x2s[:, :], cb["p_a"][:, :], x1s[:, :],
                                                op=ALU.add))

                    @blk.sync
                    def _(sy, c=c, d2=d2, dh=dh):
                        p.v.wait(sy)
                        p.io.dma(sy.dma_start(
                            out=x2_out[c * 128:(c + 1) * 128,
                                       dh * 1024 + d2 * 512: dh * 1024 + (d2 + 1) * 512],
                            in_=x2s[:, :]))

        p.barrier()

        _emit_ln_pass(p, cb, x2_out, NT, 0, score=True)
        p.barrier()

        @blk.sync
        def _(sy):
            p.io.dma(sy.dma_start(out=sc3_out[:, :], in_=cb["sc_full"][:, 0:NT]))
        p.barrier()
    nc.finalize()
    return nc


def build_L3():
    nc = bacc.Bacc("TRN2", detect_race_conditions=False)
    x2 = nc.dram_tensor("x2", [T + 1, D], F32, kind="ExternalInput").ap()
    sc3 = nc.dram_tensor("sc3", [128, 2 * NT], F32, kind="ExternalInput").ap()
    w_r = nc.dram_tensor("w_r", [1, D], F32, kind="ExternalInput").ap()
    b_r = nc.dram_tensor("b_r", [1, 1], F32, kind="ExternalInput").ap()
    wrcol_d = nc.dram_tensor("wrcol_d", [128, ND], F32, kind="ExternalInput").ap()
    w1 = nc.dram_tensor("w1", [D, DFF], F32, kind="ExternalInput").ap()
    b1col = nc.dram_tensor("b1col", [128, NFF], F32, kind="ExternalInput").ap()
    w2 = nc.dram_tensor("w2", [DFF, D], F32, kind="ExternalInput").ap()
    b2col = nc.dram_tensor("b2col", [128, ND], F32, kind="ExternalInput").ap()

    lnx3 = nc.dram_tensor("lnx3", [T + 1, D], F32, kind="Internal").ap()
    yout = nc.dram_tensor("yout", [T + 1, D], F32, kind="ExternalOutput").ap()

    with ExitStack() as st, nc.Block() as blk:
        cb = _alloc_core(nc, st)
        p = Prog(st, nc, blk)
        _emit_common_consts(p, cb)
        _emit_score_consts(p, cb, w_r, b_r)
        p.barrier()

        @blk.sync
        def _(sy):
            p.io.dma(sy.dma_start(out=cb["sc_full"][:, :], in_=sc3[:, :]))
            p.io.dma(sy.dma_start(out=lnx3[T:T + 1, :], in_=cb["zrow"][0:1, :]))
            p.io.dma(sy.dma_start(out=yout[T:T + 1, :], in_=cb["zrow"][0:1, :]))

        _emit_ln_pass(p, cb, x2, NT, 0, lnx_dst=lnx3, base_dst=yout, score=False)
        p.barrier()
        _emit_ffn(p, cb, lnx3, yout, w1, b1col, w2, b2col, wrcol_d)
        p.barrier()
    nc.finalize()
    return nc


_NC_CACHE = {}


def _get(builder, key):
    if key not in _NC_CACHE:
        _NC_CACHE[key] = builder()
    return _NC_CACHE[key]


def kernel(L0_attn_x, L0_attn_w_qkv, L0_attn_b_qkv, L0_attn_w_out, L0_attn_b_out,
           L0_ffn_W_r, L0_ffn_b_r, L0_ffn_W1, L0_ffn_b1, L0_ffn_W2, L0_ffn_b2,
           L0_ffn_position_ids, L0_ffn_cache_position):
    x = np.ascontiguousarray(np.asarray(L0_attn_x, dtype=np.float32))
    wqkvT = np.ascontiguousarray(np.asarray(L0_attn_w_qkv, np.float32).T)
    b_qkv = np.asarray(L0_attn_b_qkv, np.float32)
    bq_col = np.ascontiguousarray(b_qkv[0:D].reshape(ND, 128).T)
    bkv_row = np.ascontiguousarray(b_qkv[D:3 * D].reshape(1, 2 * D))
    woutT = np.ascontiguousarray(np.asarray(L0_attn_w_out, np.float32).T)
    b_out = np.ascontiguousarray(np.asarray(L0_attn_b_out, np.float32).reshape(1, D))
    w_r = np.ascontiguousarray(np.asarray(L0_ffn_W_r, np.float32).reshape(1, D))
    wrcol = np.ascontiguousarray(np.asarray(L0_ffn_W_r, np.float32).reshape(ND, 128).T)
    b_r = np.ascontiguousarray(np.asarray(L0_ffn_b_r, np.float32).reshape(1, 1))
    w1 = np.ascontiguousarray(np.asarray(L0_ffn_W1, np.float32))
    b1col = np.ascontiguousarray(np.asarray(L0_ffn_b1, np.float32).reshape(NFF, 128).T)
    w2 = np.ascontiguousarray(np.asarray(L0_ffn_W2, np.float32))
    b2col = np.ascontiguousarray(np.asarray(L0_ffn_b2, np.float32).reshape(ND, 128).T)

    cores = list(range(8))

    nc1 = _get(build_L1, "L1")
    maps1 = []
    for c in cores:
        b, h = c // 2, c % 2
        maps1.append({
            "x0_my": np.ascontiguousarray(x[b, h * T:(h + 1) * T, :]),
            "x0_pr": np.ascontiguousarray(x[b, (1 - h) * T:(2 - h) * T, :]),
            "wqkvT": wqkvT, "bq_col": bq_col, "bkv_row": bkv_row,
            "w_r": w_r, "b_r": b_r, "wrcol_d": wrcol,
            "w1": w1, "b1col": b1col, "w2": w2, "b2col": b2col,
        })
    r1 = run_bass_kernel_spmd(nc1, maps1, core_ids=cores).results

    nc2 = _get(build_L2, "L2")
    maps2 = []
    zs16 = np.zeros((H * HD, HD), np.float32)
    for c in cores:
        b, h = c // 2, c % 2
        maps2.append({
            "x1": r1[c]["x1"],
            "attn_in": r1[c]["attn_out"],
            "q_in": r1[c]["q_out"],
            "s16_in": (np.ascontiguousarray(r1[2 * b]["s16_out"]) if h == 1 else zs16),
            "woutT": woutT, "b_out": b_out, "w_r": w_r, "b_r": b_r,
        })
    r2 = run_bass_kernel_spmd(nc2, maps2, core_ids=cores).results

    nc3 = _get(build_L3, "L3")
    maps3 = []
    for c in cores:
        b, h = c // 2, c % 2
        prt = 2 * b + (1 - h)
        sc3v = np.concatenate([r2[c]["sc3_out"], r2[prt]["sc3_out"]], axis=1)
        maps3.append({
            "x2": r2[c]["x2_out"],
            "sc3": np.ascontiguousarray(sc3v),
            "w_r": w_r, "b_r": b_r, "wrcol_d": wrcol,
            "w1": w1, "b1col": b1col, "w2": w2, "b2col": b2col,
        })
    r3 = run_bass_kernel_spmd(nc3, maps3, core_ids=cores).results

    out = np.empty((B, 2 * T, D), np.float32)
    for c in cores:
        b, h = c // 2, c % 2
        out[b, h * T:(h + 1) * T, :] = r3[c]["yout"][0:T, :]
    return out
